# version 01
# baseline (speedup 1.0000x reference)
import contextlib
import numpy as np
import ml_dtypes
import concourse.bass as bass
import concourse.mybir as mybir
from concourse.bass_utils import run_bass_kernel_spmd

F32 = mybir.dt.float32
BF16 = mybir.dt.bfloat16
FP16 = mybir.dt.float16
FP8 = mybir.dt.float8e4
AF = mybir.ActivationFunctionType
ALU = mybir.AluOpType

NC = 8
B = 8
S = 4096
H = 1024
NT = S // 128
EXT = [8, 16, 24, 32]

C_Q4 = float(2.0 ** 49)
G_Q4 = float(1.5 * 2.0 ** 43)
C_Q5 = float(2.0 ** 42)
G_Q5 = float(1.5 * 2.0 ** 28)
EPS = 1e-5
E_NEG9 = float(np.float32(1e-9))

_ALL_PHASES = ("init", "aln", "akv", "aq", "avs", "b1a", "b1b", "b2", "b2e", "cw", "c")


def build(phases=_ALL_PHASES, debug=()):
    debug = set(debug)
    nc = bass.Bass("TRN2", target_bir_lowering=False)

    def dram(name, shape, dt, kind):
        if kind == "Internal" and name in debug:
            kind = "ExternalOutput"
        return nc.dram_tensor(name, shape, dt, kind=kind).ap()

    xall = dram("xall", [B, S, H], F32, "ExternalInput")
    xown = dram("xown", [32, 128, H], F32, "ExternalInput")
    wqt = dram("wqt", [H, H], F32, "ExternalInput")
    wkt = dram("wkt", [H, H], F32, "ExternalInput")
    wvt = dram("wvt", [H, H], F32, "ExternalInput")
    w1t = dram("w1t", [H, 4096], F32, "ExternalInput")
    w2t = dram("w2t", [4096, H], F32, "ExternalInput")
    b1d = dram("b1d", [4096], F32, "ExternalInput")
    b2d = dram("b2d", [1024], F32, "ExternalInput")
    mmul = dram("mmul", [32, 128, 512], BF16, "ExternalInput")
    madd = dram("madd", [32, 128, 512], BF16, "ExternalInput")
    d32 = dram("d32", [1024, 4096], BF16, "ExternalInput")
    d64 = dram("d64", [1024, 4096], BF16, "ExternalInput")
    identd = dram("identd", [128, 128], F32, "ExternalInput")

    lnt_s = dram("lnt_s", [B, H, S], BF16, "Internal")
    kt_s = dram("kt_s", [B, H, S], BF16, "Internal")
    v_s = dram("v_s", [B, NT, 128, H], BF16, "Internal")
    qt_s = dram("qt_s", [8, 128, B, 512], BF16, "Internal")
    e_s = dram("e_s", [B, 32, 128, 512], F32, "Internal")
    pt_s = dram("pt_s", [32, B, 128, 512], BF16, "Internal")
    x2_s = dram("x2_s", [32, 128, H], F32, "Internal")
    w2q_s = dram("w2q_s", [8, 128, 32, 128], BF16, "Internal")
    suf_h = dram("suf_h", [B, 3, 1024], F32, "Internal")
    w1q_s = dram("w1q_s", [8, 128, 4096], FP8, "Internal")
    qtb_d = dram("qtb_d", [B, 128, 4096], BF16, "Internal") if "qtb_d" in debug else None
    att_h = dram("att_h", [32, 128, H], F32, "Internal")
    suf_s = dram("suf_s", [B, 4, 1024], F32, "Internal") if "suf_s" in debug else None
    out_d = dram("out", [32, 128, H], F32, "ExternalOutput")

    st = contextlib.ExitStack()
    sems = {}
    cnt = {}
    STREAMS = {"lx": 4, "lt": 2, "lk": 2, "lv": 2, "le": 2, "sp": 3, "so": 1}
    for base, n_ in STREAMS.items():
        for i_ in range(n_):
            s_ = f"{base}{i_}"
            sems[s_] = st.enter_context(nc.semaphore(s_))
            cnt[s_] = 0
    for s_ in ("td", "vd", "sd", "gd"):
        sems[s_] = st.enter_context(nc.semaphore(s_))
        cnt[s_] = 0
    stream_idx = {b: 0 for b in STREAMS}

    block = st.enter_context(nc.Block())

    class Qu:
        def __init__(self, name):
            self.name = name
            self.fns = []

        def __call__(self, fn):
            self.fns.append(lambda e, fn=fn: fn(e))

        def wait(self, semname, val):
            if val > 0:
                self.fns.append(lambda e, s_=sems[semname], v_=val: e.wait_ge(s_, v_))

    qsy, qv, qsc, qt_, qg = Qu("sync"), Qu("vector"), Qu("scalar"), Qu("tensor"), Qu("gpsimd")
    ALLQ = (qsy, qv, qsc, qt_, qg)

    def flush():
        for q in ALLQ:
            if q.fns:
                fns, q.fns = q.fns, []
                getattr(block, q.name)(lambda eng, fns=fns: [f(eng) for f in fns])

    def inc(name, by=1):
        cnt[name] += by
        return cnt[name]

    def cop(q, semname, fn):
        v = inc(semname)
        q.fns.append(lambda e, fn=fn, s_=sems[semname]: fn(e).then_inc(s_, 1))
        return v

    def dma(q, base, fn):
        i_ = stream_idx[base] % STREAMS[base]
        stream_idx[base] += 1
        s_ = f"{base}{i_}"
        pre = cnt[s_]
        q.wait(s_, pre)
        v = inc(s_, 16)
        q.fns.append(lambda e, fn=fn, sh=sems[s_]: fn(e).then_inc(sh, 16))
        return (s_, v)

    def snap():
        return dict(cnt)

    def barrier(c0):
        for q in ALLQ:
            for k_, v_ in c0.items():
                q.wait(k_, v_)

    bankrel = [[] for _ in range(8)]

    def bank_wait(q, i):
        for tok in bankrel[i]:
            q.wait(*tok)

    def bank_set(i, *toks):
        bankrel[i] = list(toks)

    class Ring:
        def __init__(self, n):
            self.rel = [None] * n
            self.n = n

        def acquire(self, q, i):
            r = self.rel[i % self.n]
            if r:
                q.wait(*r)

        def release(self, i, a, b=None):
            self.rel[i % self.n] = a if b is None else (a, b)

    def sb(name, shape, dt):
        return st.enter_context(nc.sbuf_tensor(name, shape, dt))

    pb = [st.enter_context(nc.psum_tensor(f"pb{i}", [128, 512], F32)) for i in range(8)]

    z0 = sb("z0", [128, 1], F32)
    za = sb("za", [128, 1], F32)
    z64t = sb("z64t", [128, 1], F32)
    z32t = sb("z32t", [128, 1], F32)
    ident = sb("ident", [128, 128], F32)
    ones1 = sb("ones1", [1, 128], F32)
    onec = sb("onec", [128, 1], BF16)
    b1q = sb("b1q", [128, 32], F32)
    b2q = sb("b2q", [128, 8], F32)
    sufst = [sb(f"sufst{i}", [1, 1024], F32) for i in range(4)]
    st6 = [sb(f"st6_{i}", [128, 12], F32) for i in range(3)]
    mvt = [sb(f"mvt_{i}", [128, 2], F32) for i in range(3)]
    vet = [sb(f"vet_{i}", [128, 1], F32) for i in range(3)]
    sqt = [sb(f"sqt_{i}", [128, 1], F32) for i in range(3)]
    rst = [sb(f"rst_{i}", [128, 1], F32) for i in range(3)]
    nmt = [sb(f"nmt_{i}", [128, 1], F32) for i in range(3)]
    fa = [sb(f"fa{i}", [128, 1024], F32) for i in range(3)]
    fb = [sb(f"fb{i}", [128, 1024], F32) for i in range(2)]
    fcx = [sb(f"fc{i}", [128, 1024], F32) for i in range(2)]
    fd = [sb(f"fd{i}", [128, 1024], F32) for i in range(2)]
    fe = [sb(f"fe{i}", [128, 1024], F32) for i in range(2)]
    lnb = [sb(f"lnb{i}", [128, 1024], BF16) for i in range(3)]
    lnt = [sb(f"lnt{i}", [128, 1024], BF16) for i in range(3)]
    kb = [sb(f"kb{i}", [128, 1024], BF16) for i in range(2)]
    vbt = [sb(f"vb{i}", [128, 1024], BF16) for i in range(2)]
    qe = [sb(f"qe{i}", [128, 128], BF16) for i in range(2)]
    xq8i = sb("xq8i", [128, 40], FP8)
    qtb = [sb("qtb0", [128, 4096], BF16)]
    srt = sb("srt", [128, 1024], F32)
    mab = [sb(f"mab{i}", [128, 512], BF16) for i in range(2)]
    mmb = [sb(f"mmb{i}", [128, 512], BF16) for i in range(2)]
    ptb = [sb(f"ptb{i}", [128, 512], BF16) for i in range(2)]
    ptin = [sb(f"ptin{i}", [128, 1024], BF16) for i in range(2)]
    h5t = [sb(f"h5t{i}", [128, 1024], FP16) for i in range(2)]
    y5t = [sb(f"y5t{i}", [128, 512], FP16) for i in range(2)]
    xq8 = [sb(f"xq8{i}", [128, 1024], FP8) for i in range(2)]
    y8t = [sb(f"y8t{i}", [128, 1024], FP8) for i in range(2)]
    f8a = sb("f8a", [128, 36864], FP8)
    ew = [sb("ew0", [128, 4096], F32)]

    class LNPipe:

        def __init__(self):
            self.q = []
            self.gate = {}
            self.ring = Ring(3)

        def push(self, idx, X_ap):
            ri = idx % 3
            self.ring.acquire(qv, idx)
            qv(lambda e, X=X_ap: e.bn_stats(out=st6[ri][:, 0:6], in_=X[:, 0:512]))
            qv(lambda e, X=X_ap: e.bn_stats(out=st6[ri][:, 6:12], in_=X[:, 512:1024]))
            self.q.append(idx)
            if len(self.q) >= 2:
                self._mid(self.q.pop(0))

        def _mid(self, idx):
            ri = idx % 3
            v_ag = cop(qv, "vd", lambda e: e.bn_aggr(out=mvt[ri][:], in_=st6[ri][:]))
            qg.wait("vd", v_ag)
            g_e = cop(qg, "gd", lambda e: e.tensor_scalar(
                out=vet[ri][:], in0=mvt[ri][:, 1:2], scalar1=EPS, scalar2=None,
                op0=ALU.add))
            qsc.wait("gd", g_e)
            s_sq = cop(qsc, "sd", lambda e: e.activation(
                out=sqt[ri][:], in_=vet[ri][:], func=AF.Sqrt))
            qv.wait("sd", s_sq)
            v_rc = cop(qv, "vd", lambda e: e.reciprocal(out=rst[ri][:], in_=sqt[ri][:]))
            qg.wait("vd", v_rc)
            g_nm = cop(qg, "gd", lambda e: e.tensor_scalar(
                out=nmt[ri][:], in0=mvt[ri][:, 0:1], scalar1=rst[ri][:], scalar2=-1.0,
                op0=ALU.mult, op1=ALU.mult))
            self.gate[idx] = g_nm

        def finish(self):
            while self.q:
                self._mid(self.q.pop(0))

        def release(self, idx, semname, val):
            self.ring.release(idx, semname, val)

    def c64_chain(q, Xfn, w, Cmul, Gmin, pos, neg, s_t, c_t):
        q(lambda e: e.tensor_scalar(
            out=pos[:, 0:w], in0=Xfn(), scalar1=Cmul, scalar2=Gmin,
            op0=ALU.mult, op1=ALU.max))
        q(lambda e: e.tensor_scalar(
            out=neg[:, 0:w], in0=Xfn(), scalar1=-Cmul, scalar2=Gmin,
            op0=ALU.mult, op1=ALU.max))
        q(lambda e: e.tensor_tensor(
            out=pos[:, 0:w], in0=pos[:, 0:w], in1=neg[:, 0:w], op=ALU.max))
        q(lambda e: e.tensor_tensor(
            out=s_t[:, 0:w], in0=Xfn(), in1=pos[:, 0:w], op=ALU.add))
        q(lambda e: e.tensor_tensor(
            out=c_t[:, 0:w], in0=s_t[:, 0:w], in1=pos[:, 0:w], op=ALU.subtract))

    wkv = sb("wkv", [128, 16384], BF16)
    if "init" in phases:
        dma(qsy, "lx", lambda e: e.dma_start(out=ident[:], in_=identd[:]))
        qv(lambda e: e.memset(fa[1][:], 0.0))
        v_m = cop(qv, "vd", lambda e: e.memset(z0[:], 1.0))
        qv(lambda e: e.memset(ones1[:], 1.0))
        qv(lambda e: e.memset(onec[:], 1.0))
        qg.wait("vd", v_m)
        g_a = cop(qg, "gd", lambda e: e.tensor_scalar(
            out=za[:], in0=z0[:], scalar1=float(2.0 ** -30), scalar2=None, op0=ALU.add))
        qv.wait("gd", g_a)
        v_z64 = cop(qv, "vd", lambda e: e.tensor_scalar(
            out=z64t[:], in0=za[:], scalar1=-1.0,
            scalar2=float(2.0 ** 30), op0=ALU.add, op1=ALU.mult))
        qg.wait("vd", v_z64)
        v_z = cop(qg, "gd", lambda e: e.tensor_scalar(
            out=z32t[:], in0=z64t[:], scalar1=-1.0, scalar2=1.0,
            op0=ALU.mult, op1=ALU.add))
        dma(qsy, "lx", lambda e: e.dma_start(
            out=fa[0][:, 0:32], in_=b1d.rearrange("(c p) -> p c", p=128),
            allow_slow_non_contiguous=True))
        l_b = dma(qsy, "lx", lambda e: e.dma_start(
            out=fa[0][:, 32:40], in_=b2d.rearrange("(c p) -> p c", p=128),
            allow_slow_non_contiguous=True))
        qsc.wait(*l_b)
        s_b8 = cop(qsc, "sd", lambda e: e.activation(
            out=xq8i[:, 0:40], in_=fa[0][:, 0:40], func=AF.Identity))
        def vspace():
            qv(lambda e: e.tensor_copy(out=fcx[1][:], in_=fa[1][:]))

        qv.wait(*l_b)
        qv.wait("gd", v_z)
        qv(lambda e: e.tensor_scalar(
            out=fb[0][:, 0:40], in0=fa[0][:, 0:40], scalar1=C_Q4, scalar2=G_Q4,
            op0=ALU.mult, op1=ALU.max))
        qv(lambda e: e.tensor_scalar(
            out=fcx[0][:, 0:40], in0=fa[0][:, 0:40], scalar1=-C_Q4, scalar2=G_Q4,
            op0=ALU.mult, op1=ALU.max))
        vspace()
        qv(lambda e: e.tensor_tensor(
            out=fd[0][:, 0:40], in0=fb[0][:, 0:40], in1=fcx[0][:, 0:40], op=ALU.max))
        vspace()
        qv(lambda e: e.tensor_tensor(
            out=fd[1][:, 0:40], in0=fa[0][:, 0:40], in1=fd[0][:, 0:40], op=ALU.add))
        vspace()
        qv(lambda e: e.tensor_tensor(
            out=fb[0][:, 0:40], in0=fd[1][:, 0:40], in1=fd[0][:, 0:40], op=ALU.subtract))
        qv.wait("sd", s_b8)
        qv(lambda e: e.tensor_scalar(
            out=fcx[0][:, 0:40], in0=xq8i[:, 0:40],
            scalar1=z32t[:], scalar2=None, op0=ALU.mult))
        vspace()
        qv(lambda e: e.tensor_tensor(
            out=fd[1][:, 0:40], in0=fcx[0][:, 0:40], in1=fb[0][:, 0:40], op=ALU.add))
        vspace()
        qv(lambda e: e.tensor_copy(out=b1q[:], in_=fd[1][:, 0:32]))
        vspace()
        v_bq = cop(qv, "vd", lambda e: e.tensor_copy(out=b2q[:], in_=fd[1][:, 32:40]))
        for hc in range(8):
            lW = dma(qsy, "lx", lambda e, hc=hc: e.dma_start(
                out=fe[hc % 2][:], in_=wkt[hc * 128:(hc + 1) * 128, :]))
            qsc.wait(*lW)
            s_w = cop(qsc, "sd", lambda e, hc=hc: e.copy(
                out=wkv[:, hc * 2048:hc * 2048 + 1024], in_=fe[hc % 2][:]))
            qsy.wait("sd", s_w)
            lW2 = dma(qsy, "lx", lambda e, hc=hc: e.dma_start(
                out=fe[hc % 2][:], in_=wvt[hc * 128:(hc + 1) * 128, :]))
            qsc.wait(*lW2)
            s_w2 = cop(qsc, "sd", lambda e, hc=hc: e.copy(
                out=wkv[:, hc * 2048 + 1024:hc * 2048 + 2048], in_=fe[hc % 2][:]))
            qsy.wait("sd", s_w2)
    flush()
    c_init = snap()

    if "aln" in phases:
        barrier(c_init)
        lp = LNPipe()
        rx, rln, rlt = Ring(3), Ring(3), Ring(3)

        def aln_load(t):
            b_, tt = divmod(t, NT)
            ri3 = t % 3
            rx.acquire(qsy, t)
            lX = dma(qsy, "lx", lambda e, b_=b_, tt=tt, ri3=ri3: e.dma_start(
                out=fa[ri3][:], in_=xall[b_, tt * 128:(tt + 1) * 128, :]))
            qv.wait(*lX)
            lp.push(t, fa[ri3][:])

        def aln_norm(t):
            b_, tt = divmod(t, NT)
            ri3 = t % 3
            rln.acquire(qsc, t)
            qsc.wait("gd", lp.gate[t])
            s_no = cop(qsc, "sd", lambda e, ri3=ri3: e.activation(
                out=lnb[ri3][:], in_=fa[ri3][:], func=AF.Identity,
                scale=rst[ri3][:], bias=nmt[ri3][:]))
            rx.release(t, "sd", s_no)
            lp.release(t, "sd", s_no)
            rlt.acquire(qsy, t)
            qsy.wait("sd", s_no)
            lT = dma(qsy, "lt", lambda e, ri3=ri3: e.dma_start_transpose(
                out=lnt[ri3][:].rearrange("p (hc c) -> p hc c", c=128),
                in_=lnb[ri3][:].rearrange("p (hc c) -> p hc c", c=128)))
            rln.release(t, lT)
            qg.wait(*lT)
            sT = dma(qg, "sp", lambda e, b_=b_, tt=tt, ri3=ri3: e.dma_start(
                out=lnt_s[b_, :, tt * 128:(tt + 1) * 128].rearrange(
                    "(hc p) c -> p hc c", p=128),
                in_=lnt[ri3][:].rearrange("p (hc c) -> p hc c", c=128)))
            rlt.release(t, sT)

        NTOT = B * NT
        for t in range(NTOT):
            aln_load(t)
            if t >= 2:
                aln_norm(t - 2)
        lp.finish()
        aln_norm(NTOT - 2)
        aln_norm(NTOT - 1)
    flush()
    c_aln = snap()

    if "akv" in phases:
        barrier(c_aln)
        rlt, rkv = Ring(2), Ring(2)
        for t in range(B * NT):
            b_, tt = divmod(t, NT)
            ri = t % 2
            rlt.acquire(qsy, t)
            lL = dma(qsy, "lk", lambda e, b_=b_, tt=tt, ri=ri: e.dma_start(
                out=lnt[ri][:].rearrange("p (hc c) -> p hc c", c=128),
                in_=lnt_s[b_, :, tt * 128:(tt + 1) * 128].rearrange(
                    "(hc p) c -> p hc c", p=128)))
            g0 = (t % 2) * 4
            qt_.wait(*lL)
            for bk in range(4):
                bank_wait(qt_, g0 + bk)
            for hc in range(8):
                for ci, col in enumerate((0, 512, 1024, 1536)):
                    last = hc == 7 and ci == 3
                    fn = lambda e, ri=ri, hc=hc, col=col, g0=g0, ci=ci: e.matmul(
                        pb[g0 + ci][:], lnt[ri][:, hc * 128:(hc + 1) * 128],
                        wkv[:, hc * 2048 + col:hc * 2048 + col + 512],
                        start=(hc == 0), stop=(hc == 7))
                    if last:
                        t_kv = cop(qt_, "td", fn)
                    else:
                        qt_(fn)
            rlt.release(t, "td", t_kv)
            rkv.acquire(qsc, t)
            rkv.acquire(qv, t)
            qsc.wait("td", t_kv)
            qv.wait("td", t_kv)
            qsc(lambda e, ri=ri, g0=g0: e.copy(out=kb[ri][:, 0:512], in_=pb[g0][:]))
            s_k = cop(qsc, "sd", lambda e, ri=ri, g0=g0: e.copy(
                out=kb[ri][:, 512:1024], in_=pb[g0 + 1][:]))
            qv(lambda e, ri=ri, g0=g0: e.tensor_copy(out=vbt[ri][:, 0:512], in_=pb[g0 + 2][:]))
            v_v = cop(qv, "vd", lambda e, ri=ri, g0=g0: e.tensor_copy(
                out=vbt[ri][:, 512:1024], in_=pb[g0 + 3][:]))
            bank_set(g0, ("sd", s_k))
            bank_set(g0 + 1, ("sd", s_k))
            bank_set(g0 + 2, ("vd", v_v))
            bank_set(g0 + 3, ("vd", v_v))
            qg.wait("sd", s_k)
            qg.wait("vd", v_v)
            dma(qg, "sp", lambda e, b_=b_, tt=tt, ri=ri: e.dma_start(
                out=kt_s[b_, tt * 32:(tt + 1) * 32, :].rearrange(
                    "d (jh c) -> (d jh) c", jh=4),
                in_=kb[ri][:]))
            sKV = dma(qg, "sp", lambda e, b_=b_, tt=tt, ri=ri: e.dma_start(
                out=v_s[b_, tt], in_=vbt[ri][:]))
            rkv.release(t, sKV)
    flush()
    c_akv = snap()

    if "aq" in phases:
        barrier(c_akv)
        for hc in range(8):
            lW = dma(qsy, "lx", lambda e, hc=hc: e.dma_start(
                out=fe[hc % 2][:], in_=wqt[hc * 128:(hc + 1) * 128, :]))
            qsc.wait(*lW)
            s_w = cop(qsc, "sd", lambda e, hc=hc: e.activation(
                out=wkv[:, hc * 1024:(hc + 1) * 1024], in_=fe[hc % 2][:],
                func=AF.Identity, scale=float(1.0 / 32.0)))
            qsy.wait("sd", s_w)
        flush()
        c_wq = snap()
        barrier({"sd": c_wq["sd"]})
        rx, rln, rlt, rqe = Ring(3), Ring(3), Ring(3), Ring(2)
        lp2 = LNPipe()

        def aq_load(ot):
            ri3 = ot % 3
            rx.acquire(qsy, ot)
            lX = dma(qsy, "lx", lambda e, ot=ot, ri3=ri3: e.dma_start(
                out=fa[ri3][:], in_=xown[ot]))
            qv.wait(*lX)
            lp2.push(ot, fa[ri3][:])

        def aq_norm(ot):
            b_, k_ = divmod(ot, 4)
            ri3 = ot % 3
            rln.acquire(qsc, ot)
            qsc.wait("gd", lp2.gate[ot])
            s_no = cop(qsc, "sd", lambda e, ri3=ri3: e.activation(
                out=lnb[ri3][:], in_=fa[ri3][:], func=AF.Identity,
                scale=rst[ri3][:], bias=nmt[ri3][:]))
            rx.release(ot, "sd", s_no)
            lp2.release(ot, "sd", s_no)
            rlt.acquire(qsy, ot)
            qsy.wait("sd", s_no)
            lT = dma(qsy, "lt", lambda e, ri3=ri3: e.dma_start_transpose(
                out=lnt[ri3][:].rearrange("p (hc c) -> p hc c", c=128),
                in_=lnb[ri3][:].rearrange("p (hc c) -> p hc c", c=128)))
            rln.release(ot, lT)
            qt_.wait(*lT)
            for ds in range(8):
                bank = ds % 2
                bank_wait(qt_, bank)
                for hc in range(8):
                    fn = lambda e, ri3=ri3, ds=ds, hc=hc, bank=bank: e.matmul(
                        pb[bank][:, 0:128],
                        wkv[:, hc * 1024 + ds * 128:hc * 1024 + (ds + 1) * 128],
                        lnt[ri3][:, hc * 128:(hc + 1) * 128],
                        start=(hc == 0), stop=(hc == 7))
                    if hc == 7:
                        t_q = cop(qt_, "td", fn)
                    else:
                        qt_(fn)
                if ds == 7:
                    rlt.release(ot, "td", t_q)
                qi = ds % 2
                rqe.acquire(qv, ot * 8 + ds)
                qv.wait("td", t_q)
                v_e = cop(qv, "vd", lambda e, qi=qi, bank=bank: e.tensor_copy(
                    out=qe[qi][:, 0:128], in_=pb[bank][:, 0:128]))
                bank_set(bank, ("vd", v_e))
                qg.wait("vd", v_e)
                s_qe = dma(qg, "sp", lambda e, qi=qi, ds=ds, b_=b_, k_=k_: e.dma_start(
                    out=qt_s[ds, :, b_, k_ * 128:(k_ + 1) * 128],
                    in_=qe[qi][:, 0:128]))
                rqe.release(ot * 8 + ds, s_qe)

        for ot in range(32):
            aq_load(ot)
            if ot >= 2:
                aq_norm(ot - 2)
        lp2.finish()
        aq_norm(30)
        aq_norm(31)
    flush()
    c_aq = snap()

    if "avs" in phases:
        barrier(c_aq)
        rvb = Ring(2)
        for b_ in range(B):
            ti0 = b_ * 24
            n_ = 0
            for jt in range(8, 32):
                ti = ti0 + n_
                n_ += 1
                ri = ti % 2
                rvb.acquire(qg, ti)
                lV = dma(qg, "lv", lambda e, b_=b_, jt=jt, ri=ri: e.dma_start(
                    out=vbt[ri][:], in_=v_s[b_, jt]))
                qt_.wait(*lV)
                o = jt // 8
                fns = []
                for k_ in range(o):
                    for hf in range(2):
                        first = (jt == 8 * (k_ + 1))
                        last = (jt == 31)
                        fns.append(lambda e, ri=ri, k_=k_, hf=hf, first=first,
                                   last=last:
                                   e.matmul(pb[2 * k_ + hf][0:1, :],
                                            onec[:],
                                            vbt[ri][:, hf * 512:(hf + 1) * 512],
                                            start=first, stop=last))
                for fn in fns[:-1]:
                    qt_(fn)
                t_l = cop(qt_, "td", fns[-1])
                rvb.release(ti, "td", t_l)
            qv.wait("td", cnt["td"])
            v_e = None
            for k_ in range(3):
                qv(lambda e, k_=k_: e.tensor_copy(
                    out=sufst[k_][0:1, 0:512], in_=pb[2 * k_][0:1, :]))
                v_e = cop(qv, "vd", lambda e, k_=k_: e.tensor_copy(
                    out=sufst[k_][0:1, 512:1024], in_=pb[2 * k_ + 1][0:1, :]))
                bank_set(2 * k_, ("vd", v_e))
                bank_set(2 * k_ + 1, ("vd", v_e))
            qg.wait("vd", v_e)
            for k_ in range(3):
                g_s = cop(qg, "gd", lambda e, k_=k_: e.tensor_scalar(
                    out=sufst[3][0:1, :], in0=sufst[k_][0:1, :],
                    scalar1=E_NEG9, scalar2=None, op0=ALU.mult))
                qsy.wait("gd", g_s)
                sS = dma(qsy, "sp", lambda e, b_=b_, k_=k_: e.dma_start(
                    out=suf_h[b_, k_].rearrange("c -> () c"), in_=sufst[3][0:1, :]))
                qg.wait(*sS)
    flush()
    c_avs = snap()

    if "b1a" in phases:
        barrier(c_avs)
        rkt = Ring(2)
        rex = Ring(3)
        rqtb = Ring(1)
        for b_ in range(B):
            qb = 0
            rqtb.acquire(qsy, b_)
            lQ = dma(qsy, "lx", lambda e, b_=b_, qb=qb: e.dma_start(
                out=qtb[qb][:].rearrange("p (d i) -> p d i", i=512),
                in_=qt_s[:, :, b_, :].rearrange("d p i -> p d i")))
            if qtb_d is not None:
                qsy.wait(*lQ)
                dma(qsy, "sp", lambda e, b_=b_, qb=qb: e.dma_start(
                    out=qtb_d[b_], in_=qtb[qb][:]))
            qt_.wait(*lQ)
            for jt in range(32):
                ti = b_ * 32 + jt
                k0 = jt // 8
                w = 512 - 128 * k0
                off = 128 * k0
                ri = ti % 2
                rkt.acquire(qsy, ti)
                lK = dma(qsy, "lk", lambda e, b_=b_, jt=jt, ri=ri: e.dma_start(
                    out=kb[ri][:].rearrange("p (d j) -> p d j", j=128),
                    in_=kt_s[b_, :, jt * 128:(jt + 1) * 128].rearrange(
                        "(d p) j -> p d j", p=128)))
                bank = ti % 2
                qt_.wait(*lK)
                bank_wait(qt_, bank)
                for dc in range(8):
                    fn = lambda e, ri=ri, dc=dc, bank=bank, w=w, off=off, qb=qb: \
                        e.matmul(
                        pb[bank][:, 0:w], kb[ri][:, dc * 128:(dc + 1) * 128],
                        qtb[qb][:, dc * 512 + off:dc * 512 + off + w],
                        start=(dc == 0), stop=(dc == 7))
                    if dc == 7:
                        t_s = cop(qt_, "td", fn)
                    else:
                        qt_(fn)
                rkt.release(ti, "td", t_s)
                if jt == 31:
                    rqtb.release(b_, "td", t_s)
                ei = ti % 3
                rex.acquire(qsc, ti)
                qsc.wait("td", t_s)
                s_e = cop(qsc, "sd", lambda e, ei=ei, bank=bank, w=w: e.activation(
                    out=fa[ei][:, 0:w], in_=pb[bank][:, 0:w], func=AF.Exp))
                bank_set(bank, ("sd", s_e))
                qg.wait("sd", s_e)
                sE = dma(qg, "sp", lambda e, b_=b_, jt=jt, ei=ei, w=w: e.dma_start(
                    out=e_s[b_, jt][:, 0:w], in_=fa[ei][:, 0:w]))
                rex.release(ti, sE)
    flush()
    c_b1a = snap()

    if "b1b" in phases:
        barrier(c_b1a)
        SLOT = {
            0: dict(e=[(ew[0], 512 * b_) for b_ in range(8)],
                    l1=[(fd[1], 0), (fd[1], 512), (fe[0], 0), (fe[0], 512)],
                    l2=[(fa[1], 0), (fa[1], 512)],
                    S=(fb[0], 0), R=(srt, 0)),
            1: dict(e=[(ew[0], 512 * b_) for b_ in range(8)],
                    l1=[(fe[1], 0), (fe[1], 512), (fa[0], 0), (fa[0], 512)],
                    l2=[(fa[2], 0), (fa[2], 512)],
                    S=(fb[0], 512), R=(srt, 512)),
        }
        relo = Ring(1)
        rms = Ring(2)
        rpt = Ring(2)
        for jt in range(32):
            k0 = jt // 8
            w = 512 - 128 * k0
            off = 128 * k0
            g = jt % 2
            sl = SLOT[g]
            rms.acquire(qsy, jt)
            dma(qsy, "lk", lambda e, jt=jt, g=g, off=off, w=w: e.dma_start(
                out=mmb[g][:, 0:w], in_=mmul[jt][:, off:off + w]))
            lM = dma(qsy, "lk", lambda e, jt=jt, g=g, off=off, w=w: e.dma_start(
                out=mab[g][:, 0:w], in_=madd[jt][:, off:off + w]))
            relo.acquire(qsy, jt)
            lE = dma(qsy, "le", lambda e, jt=jt, w=w: e.dma_start(
                out=ew[0][:].rearrange("p (b i) -> p b i", i=512)[:, :, 0:w],
                in_=e_s[:, jt, :, 0:w].rearrange("b p i -> p b i")))
            qv.wait(*lE)
            v_l1 = None
            for i4 in range(4):
                t0, c0 = sl["e"][2 * i4]
                t1, c1 = sl["e"][2 * i4 + 1]
                ta, ca = sl["l1"][i4]
                fn = lambda e, t0=t0, c0=c0, t1=t1, c1=c1, ta=ta, ca=ca, w=w: \
                    e.tensor_tensor(out=ta[:, ca:ca + w], in0=t0[:, c0:c0 + w],
                                    in1=t1[:, c1:c1 + w], op=ALU.add)
                if i4 == 3:
                    v_l1 = cop(qv, "vd", fn)
                else:
                    qv(fn)
            qg.wait("vd", v_l1)
            for i2 in range(2):
                p0, q0 = sl["l1"][2 * i2]
                p1, q1 = sl["l1"][2 * i2 + 1]
                ta, ca = sl["l2"][i2]
                fn = lambda e, p0=p0, q0=q0, p1=p1, q1=q1, ta=ta, ca=ca, w=w: \
                    e.tensor_tensor(out=ta[:, ca:ca + w], in0=p0[:, q0:q0 + w],
                                    in1=p1[:, q1:q1 + w], op=ALU.add)
                if i2 == 1:
                    g_l2 = cop(qg, "gd", fn)
                else:
                    qg(fn)
            qv.wait("gd", g_l2)
            St, Sc = sl["S"]
            qv(lambda e, a=sl["l2"][0], b2_=sl["l2"][1], St=St, Sc=Sc, w=w:
               e.tensor_tensor(out=St[:, Sc:Sc + w], in0=a[0][:, a[1]:a[1] + w],
                               in1=b2_[0][:, b2_[1]:b2_[1] + w], op=ALU.add))
            Rt, Rc = sl["R"]
            qv(lambda e, St=St, Sc=Sc, Rt=Rt, Rc=Rc, w=w:
               e.reciprocal(out=Rt[:, Rc:Rc + w], in_=St[:, Sc:Sc + w]))
            qv.wait(*lM)
            qv(lambda e, Rt=Rt, Rc=Rc, St=St, Sc=Sc, g=g, w=w:
               e.tensor_tensor(out=St[:, Sc:Sc + w], in0=Rt[:, Rc:Rc + w],
                               in1=mmb[g][:, 0:w], op=ALU.mult))
            v_last = None
            for b_ in range(B):
                t_, c_ = sl["e"][b_]
                tt_, tc_ = sl["l1"][b_ % 2]
                qv(lambda e, t_=t_, c_=c_, tt_=tt_, tc_=tc_, St=St, Sc=Sc, w=w:
                   e.tensor_tensor(out=tt_[:, tc_:tc_ + w], in0=t_[:, c_:c_ + w],
                                   in1=St[:, Sc:Sc + w], op=ALU.mult))
                rpt.acquire(qv, jt * 8 + b_)
                v_last = cop(qv, "vd", lambda e, tt_=tt_, tc_=tc_, g=g, b_=b_, w=w:
                             e.tensor_tensor(out=ptb[b_ % 2][:, 0:w],
                                             in0=tt_[:, tc_:tc_ + w],
                                             in1=mab[g][:, 0:w], op=ALU.add))
                qg.wait("vd", v_last)
                sP = dma(qg, "sp", lambda e, jt=jt, b_=b_, off=off, w=w: e.dma_start(
                    out=pt_s[jt, b_][:, off:off + w], in_=ptb[b_ % 2][:, 0:w]))
                rpt.release(jt * 8 + b_, sP)
            relo.release(jt, "vd", v_last)
            rms.release(jt, "vd", v_last)
    flush()
    c_b1b = snap()

    def ln_simple(X_ap, ri, spacer_to):
        qv(lambda e: e.bn_stats(out=st6[ri][:, 0:6], in_=X_ap[:, 0:512]))
        qv(lambda e: e.bn_stats(out=st6[ri][:, 6:12], in_=X_ap[:, 512:1024]))
        qv(lambda e: e.tensor_copy(out=spacer_to[:], in_=X_ap))
        v_ag = cop(qv, "vd", lambda e: e.bn_aggr(out=mvt[ri][:], in_=st6[ri][:]))
        qg.wait("vd", v_ag)
        g_e = cop(qg, "gd", lambda e: e.tensor_scalar(
            out=vet[ri][:], in0=mvt[ri][:, 1:2], scalar1=EPS, scalar2=None,
            op0=ALU.add))
        qsc.wait("gd", g_e)
        s_sq = cop(qsc, "sd", lambda e: e.activation(
            out=sqt[ri][:], in_=vet[ri][:], func=AF.Sqrt))
        qv.wait("sd", s_sq)
        v_rc = cop(qv, "vd", lambda e: e.reciprocal(out=rst[ri][:], in_=sqt[ri][:]))
        qg.wait("vd", v_rc)
        g_nm = cop(qg, "gd", lambda e: e.tensor_scalar(
            out=nmt[ri][:], in0=mvt[ri][:, 0:1], scalar1=rst[ri][:], scalar2=-1.0,
            op0=ALU.mult, op1=ALU.mult))
        return g_nm

    if "b2" in phases:
        barrier(c_b1b)
        rv2 = Ring(2)
        rpt2 = Ring(2)
        rev = Ring(4)
        rsuf = Ring(3)
        evslots = [(fb[0], 0), (fb[0], 512), (fb[1], 0), (fb[1], 512)]
        nev = 0
        for pr in range(4):
            b0, b1 = 2 * pr, 2 * pr + 1
            for hf in range(2):
                suf_toks = []
                for k_ in range(3):
                    rsuf.acquire(qsy, k_)
                    suf_toks.append(dma(qsy, "lx", lambda e, k_=k_, b0=b0, hf=hf:
                                        e.dma_start(
                        out=sufst[k_][0:1, :].rearrange("o (b c) -> o b c", c=512),
                        in_=suf_h[b0:b0 + 2, k_,
                                  hf * 512:(hf + 1) * 512].rearrange(
                            "b c -> () b c"))))
                for jt in range(32):
                    k0 = jt // 8
                    off = 128 * k0
                    w = 512 - off
                    g = jt % 2
                    rv2.acquire(qg, jt)
                    dma(qg, "lv", lambda e, b0=b0, jt=jt, hf=hf, g=g: e.dma_start(
                        out=vbt[g][:, 0:512], in_=v_s[b0, jt][:, hf * 512:(hf + 1) * 512]))
                    lV = dma(qg, "lv", lambda e, b1=b1, jt=jt, hf=hf, g=g: e.dma_start(
                        out=vbt[g][:, 512:1024], in_=v_s[b1, jt][:, hf * 512:(hf + 1) * 512]))
                    rpt2.acquire(qsy, jt)
                    lP = dma(qsy, "le", lambda e, jt=jt, b0=b0, g=g, off=off, w=w:
                             e.dma_start(
                        out=ptin[g][:, 0:2 * w].rearrange("p (b i) -> p b i", i=w),
                        in_=pt_s[jt, b0:b0 + 2, :, off:off + w].rearrange(
                            "b p i -> p b i")))
                    qt_.wait(*lV)
                    qt_.wait(*lP)
                    if jt % 8 == 7:
                        for tk_ in suf_toks:
                            qt_.wait(*tk_)
                    fns = []
                    for bi in range(2):
                        for k_ in range(k0, 4):
                            bank = bi * 4 + k_
                            if jt == 0:
                                bank_wait(qt_, bank)
                            lo = bi * w + (k_ * 128 - off)
                            stop = (k_ == 3 and jt == 31)
                            fns.append(lambda e, g=g, bi=bi, bank=bank, lo=lo,
                                       first=(jt == 0), stop=stop: e.matmul(
                                pb[bank][:], ptin[g][:, lo:lo + 128],
                                vbt[g][:, bi * 512:(bi + 1) * 512],
                                start=first, stop=stop))
                    kc = jt // 8
                    if jt % 8 == 7:
                        for bi in range(2):
                            bank = bi * 4 + kc
                            if kc < 3:
                                fns.append(lambda e, bank=bank, kc=kc, bi=bi:
                                           e.matmul(
                                    pb[bank][:], ones1[:],
                                    sufst[kc][0:1, bi * 512:(bi + 1) * 512],
                                    start=False, stop=True))
                    for fn in fns[:-1]:
                        qt_(fn)
                    t_j = cop(qt_, "td", fns[-1])
                    rv2.release(jt, "td", t_j)
                    rpt2.release(jt, "td", t_j)
                    if jt % 8 == 7:
                        for bi, b_ in enumerate((b0, b1)):
                            bank = bi * 4 + kc
                            ot = b_ * 4 + kc
                            sl_, co_ = evslots[nev % 4]
                            rev.acquire(qv, nev)
                            qv.wait("td", t_j)
                            v_ev = cop(qv, "vd", lambda e, bank=bank, sl_=sl_,
                                       co_=co_: e.tensor_copy(
                                out=sl_[:, co_:co_ + 512], in_=pb[bank][:]))
                            bank_set(bank, ("vd", v_ev))
                            if kc < 3:
                                rsuf.release(kc, "vd", v_ev)
                            qg.wait("vd", v_ev)
                            sA = dma(qg, "sp", lambda e, ot=ot, hf=hf, sl_=sl_,
                                     co_=co_: e.dma_start(
                                out=att_h[ot][:, hf * 512:(hf + 1) * 512],
                                in_=sl_[:, co_:co_ + 512]))
                            rev.release(nev, sA)
                            nev += 1
    flush()
    c_b2 = snap()

    if "b2e" in phases:
        barrier(c_b2)
        rat = Ring(2)
        rxo = Ring(2)
        rrs = Ring(2)
        for ot in range(32):
            ri = ot % 2
            rat.acquire(qsy, ot)
            lA = dma(qsy, "lx", lambda e, ot=ot, ri=ri: e.dma_start(
                out=fa[ri][:], in_=att_h[ot]))
            rxo.acquire(qsy, ot)
            lX = dma(qsy, "lx", lambda e, ot=ot, ri=ri: e.dma_start(
                out=fcx[ri][:], in_=xown[ot]))
            qv.wait(*lA)
            g_nm = ln_simple(fa[ri][:], ri, fe[ri])
            qsc.wait("gd", g_nm)
            s_no = cop(qsc, "sd", lambda e, ri=ri: e.activation(
                out=fd[ri][:], in_=fa[ri][:], func=AF.Identity,
                scale=rst[ri][:], bias=nmt[ri][:]))
            rat.release(ot, "sd", s_no)
            qv.wait("sd", s_no)
            qv.wait(*lX)
            rrs.acquire(qv, ot)
            v_x2 = cop(qv, "vd", lambda e, ri=ri: e.tensor_tensor(
                out=fb[ri][:], in0=fd[ri][:], in1=fcx[ri][:], op=ALU.add))
            rxo.release(ot, "vd", v_x2)
            qg.wait("vd", v_x2)
            sX2 = dma(qg, "sp", lambda e, ot=ot, ri=ri: e.dma_start(
                out=x2_s[ot], in_=fb[ri][:]))
            rrs.release(ot, sX2)
    flush()
    c_b2e = snap()

    if "cw" in phases:
        barrier(c_b2e)
        rw = Ring(3)
        for i_ in range(32):
            dc, fq = divmod(i_, 4)
            ri = i_ % 3
            rw.acquire(qsy, i_)
            lW = dma(qsy, "lx", lambda e, dc=dc, fq=fq, ri=ri: e.dma_start(
                out=fa[ri][:], in_=w1t[dc * 128:(dc + 1) * 128,
                                      fq * 1024:(fq + 1) * 1024]))
            qsc.wait(*lW)
            s_w8 = cop(qsc, "sd", lambda e, ri=ri, i_=i_: e.activation(
                out=xq8[i_ % 2][:], in_=fa[ri][:], func=AF.Identity))
            qv.wait(*lW)
            qv(lambda e, ri=ri: e.tensor_scalar(
                out=fb[0][:], in0=fa[ri][:], scalar1=C_Q4, scalar2=G_Q4,
                op0=ALU.mult, op1=ALU.max))
            qv(lambda e, ri=ri: e.tensor_scalar(
                out=fb[1][:], in0=fa[ri][:], scalar1=-C_Q4, scalar2=G_Q4,
                op0=ALU.mult, op1=ALU.max))
            qv(lambda e: e.tensor_tensor(
                out=fcx[0][:], in0=fb[0][:], in1=fb[1][:], op=ALU.max))
            qv(lambda e, ri=ri: e.tensor_tensor(
                out=fcx[1][:], in0=fa[ri][:], in1=fcx[0][:], op=ALU.add))
            v_c = cop(qv, "vd", lambda e: e.tensor_tensor(
                out=fd[0][:], in0=fcx[1][:], in1=fcx[0][:], op=ALU.subtract))
            qsc.wait("vd", v_c)
            s_z = cop(qsc, "sd", lambda e, i_=i_: e.activation(
                out=fd[1][:], in_=xq8[i_ % 2][:], func=AF.Identity,
                scale=z32t[:]))
            qv.wait("sd", s_z)
            v_f = cop(qv, "vd", lambda e, i_=i_: e.tensor_tensor(
                out=xq8[i_ % 2][:], in0=fd[1][:], in1=fd[0][:], op=ALU.add))
            rw.release(i_, "vd", v_f)
            qg.wait("vd", v_f)
            sW1 = dma(qg, "sp", lambda e, dc=dc, fq=fq, i_=i_: e.dma_start(
                out=w1q_s[dc, :, fq * 1024:(fq + 1) * 1024], in_=xq8[i_ % 2][:]))
            rw.release(i_, sW1)
        rkb = Ring(2)
        for fc in range(32):
            ri = fc % 3
            rw.acquire(qsy, 32 + fc)
            lW = dma(qsy, "lx", lambda e, fc=fc, ri=ri: e.dma_start(
                out=fa[ri][:], in_=w2t[fc * 128:(fc + 1) * 128, :]))
            qsc.wait(*lW)
            s_w8 = cop(qsc, "sd", lambda e, ri=ri, fc=fc: e.activation(
                out=y8t[fc % 2][:], in_=fa[ri][:], func=AF.Identity))
            qv.wait(*lW)
            qv(lambda e, ri=ri: e.tensor_scalar(
                out=fb[0][:], in0=fa[ri][:], scalar1=C_Q4, scalar2=G_Q4,
                op0=ALU.mult, op1=ALU.max))
            qv(lambda e, ri=ri: e.tensor_scalar(
                out=fb[1][:], in0=fa[ri][:], scalar1=-C_Q4, scalar2=G_Q4,
                op0=ALU.mult, op1=ALU.max))
            qv(lambda e: e.tensor_tensor(
                out=fcx[0][:], in0=fb[0][:], in1=fb[1][:], op=ALU.max))
            qv(lambda e, ri=ri: e.tensor_tensor(
                out=fcx[1][:], in0=fa[ri][:], in1=fcx[0][:], op=ALU.add))
            v_c = cop(qv, "vd", lambda e: e.tensor_tensor(
                out=fd[0][:], in0=fcx[1][:], in1=fcx[0][:], op=ALU.subtract))
            qsc.wait("vd", v_c)
            s_z = cop(qsc, "sd", lambda e, fc=fc: e.activation(
                out=fd[1][:], in_=y8t[fc % 2][:], func=AF.Identity,
                scale=z32t[:]))
            qv.wait("sd", s_z)
            rkb.acquire(qv, fc)
            v_f = cop(qv, "vd", lambda e, fc=fc: e.tensor_tensor(
                out=kb[fc % 2][:], in0=fd[1][:], in1=fd[0][:], op=ALU.add))
            rw.release(32 + fc, "vd", v_f)
            qg.wait("vd", v_f)
            sW2 = dma(qg, "sp", lambda e, fc=fc: e.dma_start(
                out=w2q_s[:, :, fc, :].rearrange("d p c -> p d c"),
                in_=kb[fc % 2][:].rearrange("p (d c) -> p d c", c=128)))
            rkb.release(fc, sW2)
    flush()
    c_cw = snap()

    if "c" in phases or "c1" in phases or "c12" in phases:
        do_c2 = "c" in phases or "c12" in phases or "c3a" in phases or "c3b" in phases
        do_c3 = "c" in phases or "c3a" in phases or "c3b" in phases
        do_tail = "c" in phases or "c3b" in phases
        do_ln3 = "c" in phases
        barrier(c_cw)
        XQT = [lnb[0], lnb[1], lnb[2], lnt[0], lnt[1], lnt[2], kb[0], kb[1]]
        c_prev = c_cw
        for ch in range(4):
            rxc = Ring(2)
            rxq = Ring(2)
            rxq2 = Ring(2)
            for j in range(8):
                ot = ch * 8 + j
                ri = j % 2
                rxc.acquire(qsy, j)
                lX = dma(qsy, "lx", lambda e, ot=ot, ri=ri: e.dma_start(
                    out=fa[ri][:], in_=x2_s[ot]))
                qv.wait(*lX)
                g_nm = ln_simple(fa[ri][:], ri, fe[ri])
                qsc.wait("gd", g_nm)
                qsc(lambda e, ri=ri: e.activation(
                    out=xq8[ri][:], in_=fa[ri][:], func=AF.Identity,
                    scale=rst[ri][:], bias=nmt[ri][:]))
                s_ln = cop(qsc, "sd", lambda e, ri=ri: e.activation(
                    out=fb[ri][:], in_=fa[ri][:], func=AF.Identity,
                    scale=rst[ri][:], bias=nmt[ri][:]))
                rxc.release(j, "sd", s_ln)
                qv.wait("sd", s_ln)
                qv(lambda e, ri=ri: e.tensor_scalar(
                    out=fcx[0][:], in0=fb[ri][:], scalar1=C_Q4, scalar2=G_Q4,
                    op0=ALU.mult, op1=ALU.max))
                qv(lambda e, ri=ri: e.tensor_scalar(
                    out=fcx[1][:], in0=fb[ri][:], scalar1=-C_Q4, scalar2=G_Q4,
                    op0=ALU.mult, op1=ALU.max))
                qv(lambda e: e.tensor_tensor(
                    out=fd[0][:], in0=fcx[0][:], in1=fcx[1][:], op=ALU.max))
                qv(lambda e, ri=ri: e.tensor_tensor(
                    out=fd[1][:], in0=fb[ri][:], in1=fd[0][:], op=ALU.add))
                v_c4 = cop(qv, "vd", lambda e: e.tensor_tensor(
                    out=fe[1 - ri][:] if False else fcx[0][:],
                    in0=fd[1][:], in1=fd[0][:], op=ALU.subtract))
                qsc.wait("vd", v_c4)
                s_z = cop(qsc, "sd", lambda e, ri=ri: e.activation(
                    out=fcx[1][:], in_=xq8[ri][:], func=AF.Identity,
                    scale=z32t[:]))
                qv.wait("sd", s_z)
                rxq.acquire(qv, j)
                rxq2.acquire(qv, j)
                v_xq = cop(qv, "vd", lambda e, ri=ri: e.tensor_tensor(
                    out=vbt[ri][:], in0=fcx[1][:], in1=fcx[0][:], op=ALU.add))
                qsy.wait("vd", v_xq)
                ltoks = []
                for dc in range(8):
                    ltoks.append(dma(qsy, "lt", lambda e, ri=ri, dc=dc, j=j:
                                     e.dma_start_transpose(
                        out=XQT[dc][:, j * 128:(j + 1) * 128],
                        in_=vbt[ri][:, dc * 128:(dc + 1) * 128])))
                rxq.release(j, ltoks[-2])
                rxq2.release(j, ltoks[-1])
            flush()
            c_c1 = snap()
            barrier(c_c1)
            rw1 = Ring(2)
            for fs in range(32 if do_c2 else 0):
                pg = 4 + 2 * (fs % 2)
                w1o = 32768 + (fs % 2) * 1024
                rw1.acquire(qsy, fs)
                lw1 = dma(qsy, "lk", lambda e, fs=fs, w1o=w1o: e.dma_start(
                    out=f8a[:, w1o:w1o + 1024].rearrange("p (d c) -> p d c", c=128),
                    in_=w1q_s[:, :, fs * 128:(fs + 1) * 128].rearrange(
                        "d p c -> p d c")))
                bank_wait(qt_, pg)
                bank_wait(qt_, pg + 1)
                qt_.wait(*lw1)
                fns = []
                for dc in range(8):
                    for hf2 in range(2):
                        fns.append(lambda e, dc=dc, hf2=hf2, pg=pg, w1o=w1o:
                                   e.matmul(
                            pb[pg + hf2][:],
                            f8a[:, w1o + dc * 128:w1o + (dc + 1) * 128],
                            XQT[dc][:, hf2 * 512:(hf2 + 1) * 512],
                            start=(dc == 0), stop=(dc == 7)))
                for fn in fns[:-1]:
                    qt_(fn)
                t_h = cop(qt_, "td", fns[-1])
                rw1.release(fs, "td", t_h)
                qsc.wait("td", t_h)
                qsc(lambda e, pg=pg, fs=fs: e.activation(
                    out=h5t[fs % 2][:, 0:512], in_=pb[pg][:], func=AF.Identity,
                    bias=b1q[:, fs:fs + 1]))
                s_h5 = cop(qsc, "sd", lambda e, pg=pg, fs=fs: e.activation(
                    out=h5t[fs % 2][:, 512:1024], in_=pb[pg + 1][:],
                    func=AF.Identity, bias=b1q[:, fs:fs + 1]))
                qv.wait("td", t_h)
                qv.wait("sd", s_h5)
                qv(lambda e, pg=pg, fs=fs: e.tensor_scalar(
                    out=fb[0][:, 0:512], in0=pb[pg][:], scalar1=b1q[:, fs:fs + 1],
                    scalar2=None, op0=ALU.add))
                v_hf = cop(qv, "vd", lambda e, pg=pg, fs=fs: e.tensor_scalar(
                    out=fb[0][:, 512:1024], in0=pb[pg + 1][:],
                    scalar1=b1q[:, fs:fs + 1], scalar2=None, op0=ALU.add))
                bank_set(pg, ("vd", v_hf), ("sd", s_h5))
                bank_set(pg + 1, ("vd", v_hf), ("sd", s_h5))
                qv(lambda e: e.tensor_scalar(
                    out=fcx[0][:], in0=fb[0][:], scalar1=C_Q5, scalar2=G_Q5,
                    op0=ALU.mult, op1=ALU.max))
                qv(lambda e: e.tensor_scalar(
                    out=fcx[1][:], in0=fb[0][:], scalar1=-C_Q5, scalar2=G_Q5,
                    op0=ALU.mult, op1=ALU.max))
                qv(lambda e: e.tensor_tensor(
                    out=fd[0][:], in0=fcx[0][:], in1=fcx[1][:], op=ALU.max))
                qv(lambda e: e.tensor_tensor(
                    out=fd[1][:], in0=fb[0][:], in1=fd[0][:], op=ALU.add))
                v_c5 = cop(qv, "vd", lambda e: e.tensor_tensor(
                    out=fe[0][:], in0=fd[1][:], in1=fd[0][:], op=ALU.subtract))
                qsc.wait("vd", v_c5)
                s_z5 = cop(qsc, "sd", lambda e, fs=fs: e.activation(
                    out=fe[1][:], in_=h5t[fs % 2][:], func=AF.Identity,
                    scale=z32t[:]))
                qv.wait("sd", s_z5)
                v_h5q = cop(qv, "vd", lambda e: e.tensor_tensor(
                    out=fb[1][:], in0=fe[1][:], in1=fe[0][:], op=ALU.add))
                qsc.wait("vd", v_h5q)
                s_y8 = cop(qsc, "sd", lambda e, fs=fs: e.activation(
                    out=y8t[fs % 2][:], in_=fb[1][:], func=AF.Relu))
                qv(lambda e: e.tensor_scalar(
                    out=fcx[0][:], in0=fb[1][:], scalar1=0.0, scalar2=None,
                    op0=ALU.max))
                qv(lambda e: e.tensor_scalar(
                    out=fcx[1][:], in0=fcx[0][:], scalar1=C_Q4, scalar2=G_Q4,
                    op0=ALU.mult, op1=ALU.max))
                qv(lambda e: e.tensor_tensor(
                    out=fd[0][:], in0=fcx[0][:], in1=fcx[1][:], op=ALU.add))
                v_c4h = cop(qv, "vd", lambda e: e.tensor_tensor(
                    out=fd[1][:], in0=fd[0][:], in1=fcx[1][:], op=ALU.subtract))
                qsc.wait("vd", v_c4h)
                s_z4 = cop(qsc, "sd", lambda e, fs=fs: e.activation(
                    out=fe[0][:], in_=y8t[fs % 2][:], func=AF.Identity,
                    scale=z32t[:]))
                qv.wait("sd", s_z4)
                v_hq = cop(qv, "vd", lambda e, fs=fs: e.tensor_tensor(
                    out=f8a[:, fs * 1024:(fs + 1) * 1024],
                    in0=fe[0][:], in1=fd[1][:], op=ALU.add))
            flush()
            c_c2 = snap()
            barrier(c_c2)
            rw2 = Ring(2)
            rdr = Ring(2)
            for ts2 in range(2 if do_c3 else 0):
                qv.wait("so0", cnt["so0"])
                for dout in range(8):
                    g2 = dout % 2
                    rw2.acquire(qsy, ts2 * 8 + dout)
                    lw2 = dma(qsy, "lk", lambda e, dout=dout, g2=g2: e.dma_start(
                        out=wkv[:, g2 * 8192:g2 * 8192 + 4096],
                        in_=w2q_s[dout].rearrange("p f c -> p (f c)")))
                    import os as _os
                    _c3e0 = int(_os.environ.get("C3E", "2"))
                    rdr.acquire(qsy, ts2 * 8 + dout)
                    if _c3e0 >= 0:
                        dma(qsy, "lx", lambda e, dout=dout, g2=g2, ch=ch, ts2=ts2:
                            e.dma_start(
                            out=mmb[g2][:],
                            in_=d32[dout * 128:(dout + 1) * 128,
                                    ch * 1024 + ts2 * 512:ch * 1024 + ts2 * 512 + 512]))
                        lDr = dma(qsy, "lx", lambda e, dout=dout, g2=g2, ch=ch, ts2=ts2:
                                  e.dma_start(
                            out=mab[g2][:],
                            in_=d64[dout * 128:(dout + 1) * 128,
                                    ch * 1024 + ts2 * 512:ch * 1024 + ts2 * 512 + 512]))
                    qt_.wait(*lw2)
                    bank_wait(qt_, g2)
                    fns = []
                    for fc in range(32):
                        fns.append(lambda e, fc=fc, g2=g2, ts2=ts2: e.matmul(
                            pb[g2][:],
                            wkv[:, g2 * 8192 + fc * 128:g2 * 8192 + (fc + 1) * 128],
                            f8a[:, fc * 1024 + ts2 * 512:
                                fc * 1024 + ts2 * 512 + 512],
                            start=(fc == 0), stop=(fc == 31)))
                    for fn in fns[:-1]:
                        qt_(fn)
                    t_y = cop(qt_, "td", fns[-1])
                    rw2.release(ts2 * 8 + dout, "td", t_y)
                    import os as _os
                    _c3e = int(_os.environ.get("C3E", "2"))
                    if _c3e0 <= 0:
                        qv.wait("td", t_y)
                        v_mn = cop(qv, "vd", lambda e, g2=g2: e.tensor_copy(
                            out=fb[0][:, 0:512], in_=pb[g2][:]))
                        bank_set(g2, ("vd", v_mn))
                        v_yd = cop(qv, "vd", lambda e, dout=dout: e.tensor_copy(
                            out=ew[0][:, dout * 512:(dout + 1) * 512],
                            in_=fb[0][:, 0:512]))
                        rdr.release(ts2 * 8 + dout, "vd", v_yd)
                        continue
                    if _c3e0 == 1:
                        qsc.wait("td", t_y)
                        s_y5 = cop(qsc, "sd", lambda e, g2=g2, dout=dout: e.activation(
                            out=y5t[g2][:], in_=pb[g2][:], func=AF.Identity,
                            bias=b2q[:, dout:dout + 1]))
                        qv.wait("td", t_y)
                        qv.wait("sd", s_y5)
                        v_yf = cop(qv, "vd", lambda e, g2=g2, dout=dout: e.tensor_scalar(
                            out=fb[0][:, 0:512], in0=pb[g2][:],
                            scalar1=b2q[:, dout:dout + 1], scalar2=None, op0=ALU.add))
                        bank_set(g2, ("vd", v_yf), ("sd", s_y5))
                        v_yd = cop(qv, "vd", lambda e, dout=dout: e.tensor_copy(
                            out=ew[0][:, dout * 512:(dout + 1) * 512],
                            in_=fb[0][:, 0:512]))
                        rdr.release(ts2 * 8 + dout, "vd", v_yd)
                        continue
                    qsc.wait("td", t_y)
                    s_y5 = cop(qsc, "sd", lambda e, g2=g2, dout=dout: e.activation(
                        out=y5t[g2][:], in_=pb[g2][:], func=AF.Identity,
                        bias=b2q[:, dout:dout + 1]))
                    qv.wait("td", t_y)
                    qv.wait("sd", s_y5)
                    v_yf = cop(qv, "vd", lambda e, g2=g2, dout=dout: e.tensor_scalar(
                        out=fb[0][:, 0:512], in0=pb[g2][:],
                        scalar1=b2q[:, dout:dout + 1], scalar2=None, op0=ALU.add))
                    bank_set(g2, ("vd", v_yf), ("sd", s_y5))
                    if _c3e == 0:
                        v_yd = cop(qv, "vd", lambda e, dout=dout: e.tensor_copy(
                            out=ew[0][:, dout * 512:(dout + 1) * 512],
                            in_=fb[0][:, 0:512]))
                        rdr.release(ts2 * 8 + dout, "vd", v_yd)
                        continue
                    qv(lambda e: e.tensor_scalar(
                        out=fcx[0][:, 0:512], in0=fb[0][:, 0:512], scalar1=C_Q5,
                        scalar2=G_Q5, op0=ALU.mult, op1=ALU.max))
                    qv(lambda e: e.tensor_scalar(
                        out=fcx[1][:, 0:512], in0=fb[0][:, 0:512], scalar1=-C_Q5,
                        scalar2=G_Q5, op0=ALU.mult, op1=ALU.max))
                    qv(lambda e: e.tensor_tensor(
                        out=fd[0][:, 0:512], in0=fcx[0][:, 0:512],
                        in1=fcx[1][:, 0:512], op=ALU.max))
                    qv(lambda e: e.tensor_tensor(
                        out=fd[1][:, 0:512], in0=fb[0][:, 0:512],
                        in1=fd[0][:, 0:512], op=ALU.add))
                    v_c5y = cop(qv, "vd", lambda e: e.tensor_tensor(
                        out=fe[0][:, 0:512], in0=fd[1][:, 0:512],
                        in1=fd[0][:, 0:512], op=ALU.subtract))
                    qsc.wait("vd", v_c5y)
                    s_zy = cop(qsc, "sd", lambda e, g2=g2: e.activation(
                        out=fe[1][:, 0:512], in_=y5t[g2][:], func=AF.Identity,
                        scale=z32t[:]))
                    qv.wait("sd", s_zy)
                    qv(lambda e: e.tensor_tensor(
                        out=fb[1][:, 0:512], in0=fe[1][:, 0:512],
                        in1=fe[0][:, 0:512], op=ALU.add))
                    qv.wait(*lDr)
                    qv(lambda e, g2=g2: e.tensor_scalar(
                        out=fcx[0][:, 0:512], in0=mmb[g2][:], scalar1=z32t[:],
                        scalar2=None, op0=ALU.mult))
                    qv(lambda e, g2=g2: e.tensor_scalar(
                        out=fcx[1][:, 0:512], in0=mab[g2][:], scalar1=z64t[:],
                        scalar2=None, op0=ALU.mult))
                    qv(lambda e: e.tensor_tensor(
                        out=fd[0][:, 0:512], in0=fcx[0][:, 0:512],
                        in1=fcx[1][:, 0:512], op=ALU.add))
                    v_yd = cop(qv, "vd", lambda e, dout=dout: e.tensor_tensor(
                        out=ew[0][:, dout * 512:(dout + 1) * 512],
                        in0=fb[1][:, 0:512], in1=fd[0][:, 0:512], op=ALU.mult))
                    rdr.release(ts2 * 8 + dout, "vd", v_yd)
                v_all = cnt["vd"]
                v_o_prev = 0
                so_hist = [("so0", 0), ("so0", 0)]
                for tt in range(4 if do_tail else 0):
                    ot = ch * 8 + ts2 * 4 + tt
                    rt2 = tt % 2
                    qt_.wait("vd", v_all)
                    for dout in range(8):
                        tp = 2 + (dout % 2)
                        bank_wait(qt_, tp)
                        t_tr = cop(qt_, "td", lambda e, tp=tp, dout=dout, tt=tt:
                                   e.transpose(
                            pb[tp][:, 0:128],
                            ew[0][:, dout * 512 + tt * 128:dout * 512 + tt * 128 + 128],
                            ident[:]))
                        qv.wait("td", t_tr)
                        v_as = cop(qv, "vd", lambda e, tp=tp, dout=dout, rt2=rt2:
                                   e.tensor_copy(
                            out=fa[rt2][:, dout * 128:(dout + 1) * 128],
                            in_=pb[tp][:, 0:128]))
                        bank_set(tp, ("vd", v_as))
                    if not do_ln3:
                        continue
                    g_nm = ln_simple(fa[rt2][:], rt2, fe[rt2])
                    qsy.wait("vd", cnt["vd"])
                    lX2 = dma(qsy, "lx", lambda e, ot=ot, rt2=rt2: e.dma_start(
                        out=fd[rt2][:], in_=x2_s[ot]))
                    qsc.wait("gd", g_nm)
                    qsc.wait("vd", v_o_prev)
                    s_n3 = cop(qsc, "sd", lambda e, rt2=rt2: e.activation(
                        out=srt[:], in_=fa[rt2][:], func=AF.Identity,
                        scale=rst[rt2][:], bias=nmt[rt2][:]))
                    qv.wait("sd", s_n3)
                    qv.wait(*lX2)
                    qv.wait(*so_hist[tt % 2])
                    v_o = cop(qv, "vd", lambda e, rt2=rt2: e.tensor_tensor(
                        out=fb[rt2][:], in0=srt[:], in1=fd[rt2][:], op=ALU.add))
                    v_o_prev = v_o
                    qg.wait("vd", v_o)
                    sO = dma(qg, "so", lambda e, ot=ot, rt2=rt2: e.dma_start(
                        out=out_d[ot], in_=fb[rt2][:]))
                    so_hist[tt % 2] = sO
            flush()
            c_prev = snap()
            barrier(c_prev)
    flush()
    c_c = snap()

    for i_ in range(STREAMS["sp"]):
        qsy.wait(f"sp{i_}", cnt[f"sp{i_}"])
    qsy.wait("so0", cnt["so0"])
    flush()
    st.close()
    return nc


def _np_masks():
    import jax
    import jax.numpy as jnp
    with jax.default_device(jax.devices("cpu")[0]):
        key = jax.random.key(42, impl="threefry2x32")
        keep32 = np.asarray(jax.random.bernoulli(key, 0.8, (B, S, H)))
        old = jax.config.jax_enable_x64
        try:
            jax.config.update("jax_enable_x64", True)
            k2 = jax.random.key(42, impl="threefry2x32")
            u64 = np.asarray(jax.random.uniform(k2, (B, S, H), dtype=jnp.float64))
        finally:
            jax.config.update("jax_enable_x64", old)
        keep64 = u64 < 0.8
    return keep32, keep64


def _stage(core, x, Wq, Wk, Wv, w1, b1, b2_, w2, masks):
    keep32, keep64 = masks
    d = {}
    d["xall"] = x
    ot_rows = []
    for ot in range(32):
        b_, k_ = divmod(ot, 4)
        sk = 8 * k_ + core
        ot_rows.append(x[b_, 128 * sk:128 * (sk + 1), :])
    d["xown"] = np.ascontiguousarray(np.stack(ot_rows))
    d["wqt"] = np.ascontiguousarray(Wq.T)
    d["wkt"] = np.ascontiguousarray(Wk.T)
    d["wvt"] = np.ascontiguousarray(Wv.T)
    d["w1t"] = np.ascontiguousarray(w1.T)
    d["w2t"] = np.ascontiguousarray(w2.T)
    d["b1d"] = np.ascontiguousarray(b1)
    d["b2d"] = np.ascontiguousarray(b2_)
    jj = np.arange(128)
    tril_t = (jj[None, :] >= jj[:, None]).astype(np.float32)
    mm = np.zeros((32, 128, 512), np.float32)
    ma = np.zeros((32, 128, 512), np.float32)
    for jt in range(32):
        for k_ in range(4):
            sk = 8 * k_ + core
            if jt < sk:
                mm[jt, :, k_ * 128:(k_ + 1) * 128] = 1.0
            elif jt == sk:
                mm[jt, :, k_ * 128:(k_ + 1) * 128] = tril_t
                ma[jt, :, k_ * 128:(k_ + 1) * 128] = np.float32(1e-9) * (1.0 - tril_t)
            else:
                ma[jt, :, k_ * 128:(k_ + 1) * 128] = np.float32(1e-9)
    d["mmul"] = mm.astype(ml_dtypes.bfloat16)
    d["madd"] = ma.astype(ml_dtypes.bfloat16)

    def gather(keep):
        rows = []
        for ot in range(32):
            b_, k_ = divmod(ot, 4)
            sk = 8 * k_ + core
            rows.append(keep[b_, 128 * sk:128 * (sk + 1), :])
        g = np.concatenate(rows, axis=0)
        return np.ascontiguousarray(
            (g.T.astype(np.float32) * 1.25).astype(ml_dtypes.bfloat16))

    d["d32"] = gather(keep32)
    d["d64"] = gather(keep64)
    d["identd"] = np.eye(128, dtype=np.float32)
    return d


def kernel(L0_attn_x, L0_attn_Wq, L0_attn_Wk, L0_attn_Wv,
           L0_ffn_w1, L0_ffn_b1, L0_ffn_w2, L0_ffn_b2):
    x = np.ascontiguousarray(np.asarray(L0_attn_x, np.float32))
    masks = _np_masks()
    in_maps = [_stage(c, x, np.asarray(L0_attn_Wq, np.float32),
                      np.asarray(L0_attn_Wk, np.float32),
                      np.asarray(L0_attn_Wv, np.float32),
                      np.asarray(L0_ffn_w1, np.float32),
                      np.asarray(L0_ffn_b1, np.float32),
                      np.asarray(L0_ffn_b2, np.float32),
                      np.asarray(L0_ffn_w2, np.float32), masks)
               for c in range(NC)]
    nc = build()
    res = run_bass_kernel_spmd(nc, in_maps, core_ids=list(range(NC)))
    out = np.empty((B, S, H), np.float32)
    for c in range(NC):
        oc = np.asarray(res.results[c]["out"])
        for ot in range(32):
            b_, k_ = divmod(ot, 4)
            sk = 8 * k_ + c
            out[b_, 128 * sk:128 * (sk + 1), :] = oc[ot]
    return out


# version 02 (final)
# speedup vs baseline: 1.0015x; 1.0015x over previous
import contextlib
import numpy as np
import ml_dtypes
import concourse.bass as bass
import concourse.mybir as mybir
from concourse.bass_utils import run_bass_kernel_spmd

F32 = mybir.dt.float32
BF16 = mybir.dt.bfloat16
FP16 = mybir.dt.float16
FP8 = mybir.dt.float8e4
AF = mybir.ActivationFunctionType
ALU = mybir.AluOpType

NC = 8
B = 8
S = 4096
H = 1024
NT = S // 128
EXT = [8, 16, 24, 32]

C_Q4 = float(2.0 ** 49)
G_Q4 = float(1.5 * 2.0 ** 43)
C_Q5 = float(2.0 ** 42)
G_Q5 = float(1.5 * 2.0 ** 28)
EPS = 1e-5
E_NEG9 = float(np.float32(1e-9))

_ALL_PHASES = ("init", "aln", "akv", "aq", "avs", "b1a", "b1b", "b2", "b2e", "cw", "c")


def build(phases=_ALL_PHASES, debug=()):
    debug = set(debug)
    nc = bass.Bass("TRN2", target_bir_lowering=False,
                   detect_race_conditions=False)

    def dram(name, shape, dt, kind):
        if kind == "Internal" and name in debug:
            kind = "ExternalOutput"
        return nc.dram_tensor(name, shape, dt, kind=kind).ap()

    xall = dram("xall", [B, S, H], F32, "ExternalInput")
    xown = dram("xown", [32, 128, H], F32, "ExternalInput")
    wqt = dram("wqt", [H, H], F32, "ExternalInput")
    wkt = dram("wkt", [H, H], F32, "ExternalInput")
    wvt = dram("wvt", [H, H], F32, "ExternalInput")
    w1t = dram("w1t", [H, 4096], F32, "ExternalInput")
    w2t = dram("w2t", [4096, H], F32, "ExternalInput")
    b1d = dram("b1d", [4096], F32, "ExternalInput")
    b2d = dram("b2d", [1024], F32, "ExternalInput")
    mmul = dram("mmul", [32, 128, 512], BF16, "ExternalInput")
    madd = dram("madd", [32, 128, 512], BF16, "ExternalInput")
    d32 = dram("d32", [1024, 4096], BF16, "ExternalInput")
    d64 = dram("d64", [1024, 4096], BF16, "ExternalInput")
    identd = dram("identd", [128, 128], F32, "ExternalInput")

    lnt_s = dram("lnt_s", [B, H, S], BF16, "Internal")
    kt_s = dram("kt_s", [B, H, S], BF16, "Internal")
    v_s = dram("v_s", [B, NT, 128, H], BF16, "Internal")
    qt_s = dram("qt_s", [8, 128, B, 512], BF16, "Internal")
    e_s = dram("e_s", [B, 32, 128, 512], F32, "Internal")
    pt_s = dram("pt_s", [32, B, 128, 512], BF16, "Internal")
    x2_s = dram("x2_s", [32, 128, H], F32, "Internal")
    w2q_s = dram("w2q_s", [8, 128, 32, 128], BF16, "Internal")
    suf_h = dram("suf_h", [B, 3, 1024], F32, "Internal")
    w1q_s = dram("w1q_s", [8, 128, 4096], FP8, "Internal")
    qtb_d = dram("qtb_d", [B, 128, 4096], BF16, "Internal") if "qtb_d" in debug else None
    att_h = dram("att_h", [32, 128, H], F32, "Internal")
    suf_s = dram("suf_s", [B, 4, 1024], F32, "Internal") if "suf_s" in debug else None
    out_d = dram("out", [32, 128, H], F32, "ExternalOutput")

    st = contextlib.ExitStack()
    sems = {}
    cnt = {}
    STREAMS = {"lx": 4, "lt": 2, "lk": 2, "lv": 2, "le": 2, "sp": 3, "so": 1}
    for base, n_ in STREAMS.items():
        for i_ in range(n_):
            s_ = f"{base}{i_}"
            sems[s_] = st.enter_context(nc.semaphore(s_))
            cnt[s_] = 0
    for s_ in ("td", "vd", "sd", "gd"):
        sems[s_] = st.enter_context(nc.semaphore(s_))
        cnt[s_] = 0
    stream_idx = {b: 0 for b in STREAMS}

    block = st.enter_context(nc.Block())

    class Qu:
        def __init__(self, name):
            self.name = name
            self.fns = []

        def __call__(self, fn):
            self.fns.append(lambda e, fn=fn: fn(e))

        def wait(self, semname, val):
            if val > 0:
                self.fns.append(lambda e, s_=sems[semname], v_=val: e.wait_ge(s_, v_))

    qsy, qv, qsc, qt_, qg = Qu("sync"), Qu("vector"), Qu("scalar"), Qu("tensor"), Qu("gpsimd")
    ALLQ = (qsy, qv, qsc, qt_, qg)

    def flush():
        for q in ALLQ:
            if q.fns:
                fns, q.fns = q.fns, []
                getattr(block, q.name)(lambda eng, fns=fns: [f(eng) for f in fns])

    def inc(name, by=1):
        cnt[name] += by
        return cnt[name]

    def cop(q, semname, fn):
        v = inc(semname)
        q.fns.append(lambda e, fn=fn, s_=sems[semname]: fn(e).then_inc(s_, 1))
        return v

    def dma(q, base, fn):
        i_ = stream_idx[base] % STREAMS[base]
        stream_idx[base] += 1
        s_ = f"{base}{i_}"
        pre = cnt[s_]
        q.wait(s_, pre)
        v = inc(s_, 16)
        q.fns.append(lambda e, fn=fn, sh=sems[s_]: fn(e).then_inc(sh, 16))
        return (s_, v)

    def snap():
        return dict(cnt)

    def barrier(c0):
        for q in ALLQ:
            for k_, v_ in c0.items():
                q.wait(k_, v_)

    bankrel = [[] for _ in range(8)]

    def bank_wait(q, i):
        for tok in bankrel[i]:
            q.wait(*tok)

    def bank_set(i, *toks):
        bankrel[i] = list(toks)

    class Ring:
        def __init__(self, n):
            self.rel = [None] * n
            self.n = n

        def acquire(self, q, i):
            r = self.rel[i % self.n]
            if r:
                q.wait(*r)

        def release(self, i, a, b=None):
            self.rel[i % self.n] = a if b is None else (a, b)

    def sb(name, shape, dt):
        return st.enter_context(nc.sbuf_tensor(name, shape, dt))

    pb = [st.enter_context(nc.psum_tensor(f"pb{i}", [128, 512], F32)) for i in range(8)]

    z0 = sb("z0", [128, 1], F32)
    za = sb("za", [128, 1], F32)
    z64t = sb("z64t", [128, 1], F32)
    z32t = sb("z32t", [128, 1], F32)
    ident = sb("ident", [128, 128], F32)
    ones1 = sb("ones1", [1, 128], F32)
    onec = sb("onec", [128, 1], BF16)
    b1q = sb("b1q", [128, 32], F32)
    b2q = sb("b2q", [128, 8], F32)
    sufst = [sb(f"sufst{i}", [1, 1024], F32) for i in range(4)]
    st6 = [sb(f"st6_{i}", [128, 12], F32) for i in range(3)]
    mvt = [sb(f"mvt_{i}", [128, 2], F32) for i in range(3)]
    vet = [sb(f"vet_{i}", [128, 1], F32) for i in range(3)]
    sqt = [sb(f"sqt_{i}", [128, 1], F32) for i in range(3)]
    rst = [sb(f"rst_{i}", [128, 1], F32) for i in range(3)]
    nmt = [sb(f"nmt_{i}", [128, 1], F32) for i in range(3)]
    fa = [sb(f"fa{i}", [128, 1024], F32) for i in range(3)]
    fb = [sb(f"fb{i}", [128, 1024], F32) for i in range(2)]
    fcx = [sb(f"fc{i}", [128, 1024], F32) for i in range(2)]
    fd = [sb(f"fd{i}", [128, 1024], F32) for i in range(2)]
    fe = [sb(f"fe{i}", [128, 1024], F32) for i in range(2)]
    lnb = [sb(f"lnb{i}", [128, 1024], BF16) for i in range(3)]
    lnt = [sb(f"lnt{i}", [128, 1024], BF16) for i in range(3)]
    kb = [sb(f"kb{i}", [128, 1024], BF16) for i in range(2)]
    vbt = [sb(f"vb{i}", [128, 1024], BF16) for i in range(2)]
    qe = [sb(f"qe{i}", [128, 128], BF16) for i in range(2)]
    xq8i = sb("xq8i", [128, 40], FP8)
    qtb = [sb("qtb0", [128, 4096], BF16)]
    srt = sb("srt", [128, 1024], F32)
    mab = [sb(f"mab{i}", [128, 512], BF16) for i in range(2)]
    mmb = [sb(f"mmb{i}", [128, 512], BF16) for i in range(2)]
    ptb = [sb(f"ptb{i}", [128, 512], BF16) for i in range(2)]
    ptin = [sb(f"ptin{i}", [128, 1024], BF16) for i in range(2)]
    h5t = [sb(f"h5t{i}", [128, 1024], FP16) for i in range(2)]
    y5t = [sb(f"y5t{i}", [128, 512], FP16) for i in range(2)]
    xq8 = [sb(f"xq8{i}", [128, 1024], FP8) for i in range(2)]
    y8t = [sb(f"y8t{i}", [128, 1024], FP8) for i in range(2)]
    f8a = sb("f8a", [128, 36864], FP8)
    ew = [sb("ew0", [128, 4096], F32)]

    class LNPipe:

        def __init__(self):
            self.q = []
            self.gate = {}
            self.ring = Ring(3)

        def push(self, idx, X_ap):
            ri = idx % 3
            self.ring.acquire(qv, idx)
            qv(lambda e, X=X_ap: e.bn_stats(out=st6[ri][:, 0:6], in_=X[:, 0:512]))
            qv(lambda e, X=X_ap: e.bn_stats(out=st6[ri][:, 6:12], in_=X[:, 512:1024]))
            self.q.append(idx)
            if len(self.q) >= 2:
                self._mid(self.q.pop(0))

        def _mid(self, idx):
            ri = idx % 3
            v_ag = cop(qv, "vd", lambda e: e.bn_aggr(out=mvt[ri][:], in_=st6[ri][:]))
            qg.wait("vd", v_ag)
            g_e = cop(qg, "gd", lambda e: e.tensor_scalar(
                out=vet[ri][:], in0=mvt[ri][:, 1:2], scalar1=EPS, scalar2=None,
                op0=ALU.add))
            qsc.wait("gd", g_e)
            s_sq = cop(qsc, "sd", lambda e: e.activation(
                out=sqt[ri][:], in_=vet[ri][:], func=AF.Sqrt))
            qv.wait("sd", s_sq)
            v_rc = cop(qv, "vd", lambda e: e.reciprocal(out=rst[ri][:], in_=sqt[ri][:]))
            qg.wait("vd", v_rc)
            g_nm = cop(qg, "gd", lambda e: e.tensor_scalar(
                out=nmt[ri][:], in0=mvt[ri][:, 0:1], scalar1=rst[ri][:], scalar2=-1.0,
                op0=ALU.mult, op1=ALU.mult))
            self.gate[idx] = g_nm

        def finish(self):
            while self.q:
                self._mid(self.q.pop(0))

        def release(self, idx, semname, val):
            self.ring.release(idx, semname, val)

    def c64_chain(q, Xfn, w, Cmul, Gmin, pos, neg, s_t, c_t):
        q(lambda e: e.tensor_scalar(
            out=pos[:, 0:w], in0=Xfn(), scalar1=Cmul, scalar2=Gmin,
            op0=ALU.mult, op1=ALU.max))
        q(lambda e: e.tensor_scalar(
            out=neg[:, 0:w], in0=Xfn(), scalar1=-Cmul, scalar2=Gmin,
            op0=ALU.mult, op1=ALU.max))
        q(lambda e: e.tensor_tensor(
            out=pos[:, 0:w], in0=pos[:, 0:w], in1=neg[:, 0:w], op=ALU.max))
        q(lambda e: e.tensor_tensor(
            out=s_t[:, 0:w], in0=Xfn(), in1=pos[:, 0:w], op=ALU.add))
        q(lambda e: e.tensor_tensor(
            out=c_t[:, 0:w], in0=s_t[:, 0:w], in1=pos[:, 0:w], op=ALU.subtract))

    wkv = sb("wkv", [128, 16384], BF16)
    if "init" in phases:
        dma(qsy, "lx", lambda e: e.dma_start(out=ident[:], in_=identd[:]))
        qv(lambda e: e.memset(fa[1][:], 0.0))
        v_m = cop(qv, "vd", lambda e: e.memset(z0[:], 1.0))
        qv(lambda e: e.memset(ones1[:], 1.0))
        qv(lambda e: e.memset(onec[:], 1.0))
        qg.wait("vd", v_m)
        g_a = cop(qg, "gd", lambda e: e.tensor_scalar(
            out=za[:], in0=z0[:], scalar1=float(2.0 ** -30), scalar2=None, op0=ALU.add))
        qv.wait("gd", g_a)
        v_z64 = cop(qv, "vd", lambda e: e.tensor_scalar(
            out=z64t[:], in0=za[:], scalar1=-1.0,
            scalar2=float(2.0 ** 30), op0=ALU.add, op1=ALU.mult))
        qg.wait("vd", v_z64)
        v_z = cop(qg, "gd", lambda e: e.tensor_scalar(
            out=z32t[:], in0=z64t[:], scalar1=-1.0, scalar2=1.0,
            op0=ALU.mult, op1=ALU.add))
        l_b1 = dma(qsy, "lx", lambda e: e.dma_start(
            out=fa[0][:, 0:32], in_=b1d.rearrange("(c p) -> p c", p=128),
            allow_slow_non_contiguous=True))
        l_b = dma(qsy, "lx", lambda e: e.dma_start(
            out=fa[0][:, 32:40], in_=b2d.rearrange("(c p) -> p c", p=128),
            allow_slow_non_contiguous=True))
        qsc.wait(*l_b1)
        qsc.wait(*l_b)
        s_b8 = cop(qsc, "sd", lambda e: e.activation(
            out=xq8i[:, 0:40], in_=fa[0][:, 0:40], func=AF.Identity))
        def vspace():
            qv(lambda e: e.tensor_copy(out=fcx[1][:], in_=fa[1][:]))

        qv.wait(*l_b1)
        qv.wait(*l_b)
        qv.wait("gd", v_z)
        qv(lambda e: e.tensor_scalar(
            out=fb[0][:, 0:40], in0=fa[0][:, 0:40], scalar1=C_Q4, scalar2=G_Q4,
            op0=ALU.mult, op1=ALU.max))
        qv(lambda e: e.tensor_scalar(
            out=fcx[0][:, 0:40], in0=fa[0][:, 0:40], scalar1=-C_Q4, scalar2=G_Q4,
            op0=ALU.mult, op1=ALU.max))
        vspace()
        qv(lambda e: e.tensor_tensor(
            out=fd[0][:, 0:40], in0=fb[0][:, 0:40], in1=fcx[0][:, 0:40], op=ALU.max))
        vspace()
        qv(lambda e: e.tensor_tensor(
            out=fd[1][:, 0:40], in0=fa[0][:, 0:40], in1=fd[0][:, 0:40], op=ALU.add))
        vspace()
        qv(lambda e: e.tensor_tensor(
            out=fb[0][:, 0:40], in0=fd[1][:, 0:40], in1=fd[0][:, 0:40], op=ALU.subtract))
        qv.wait("sd", s_b8)
        qv(lambda e: e.tensor_scalar(
            out=fcx[0][:, 0:40], in0=xq8i[:, 0:40],
            scalar1=z32t[:], scalar2=None, op0=ALU.mult))
        vspace()
        qv(lambda e: e.tensor_tensor(
            out=fd[1][:, 0:40], in0=fcx[0][:, 0:40], in1=fb[0][:, 0:40], op=ALU.add))
        vspace()
        qv(lambda e: e.tensor_copy(out=b1q[:], in_=fd[1][:, 0:32]))
        vspace()
        v_bq = cop(qv, "vd", lambda e: e.tensor_copy(out=b2q[:], in_=fd[1][:, 32:40]))
        for hc in range(8):
            lW = dma(qsy, "lx", lambda e, hc=hc: e.dma_start(
                out=fe[hc % 2][:], in_=wkt[hc * 128:(hc + 1) * 128, :]))
            qsc.wait(*lW)
            s_w = cop(qsc, "sd", lambda e, hc=hc: e.copy(
                out=wkv[:, hc * 2048:hc * 2048 + 1024], in_=fe[hc % 2][:]))
            qsy.wait("sd", s_w)
            lW2 = dma(qsy, "lx", lambda e, hc=hc: e.dma_start(
                out=fe[hc % 2][:], in_=wvt[hc * 128:(hc + 1) * 128, :]))
            qsc.wait(*lW2)
            s_w2 = cop(qsc, "sd", lambda e, hc=hc: e.copy(
                out=wkv[:, hc * 2048 + 1024:hc * 2048 + 2048], in_=fe[hc % 2][:]))
            qsy.wait("sd", s_w2)
    flush()
    c_init = snap()

    if "aln" in phases:
        barrier(c_init)
        lp = LNPipe()
        rx, rln, rlt = Ring(3), Ring(3), Ring(3)

        def aln_load(t):
            b_, tt = divmod(t, NT)
            ri3 = t % 3
            rx.acquire(qsy, t)
            lX = dma(qsy, "lx", lambda e, b_=b_, tt=tt, ri3=ri3: e.dma_start(
                out=fa[ri3][:], in_=xall[b_, tt * 128:(tt + 1) * 128, :]))
            qv.wait(*lX)
            lp.push(t, fa[ri3][:])

        def aln_norm(t):
            b_, tt = divmod(t, NT)
            ri3 = t % 3
            rln.acquire(qsc, t)
            qsc.wait("gd", lp.gate[t])
            s_no = cop(qsc, "sd", lambda e, ri3=ri3: e.activation(
                out=lnb[ri3][:], in_=fa[ri3][:], func=AF.Identity,
                scale=rst[ri3][:], bias=nmt[ri3][:]))
            rx.release(t, "sd", s_no)
            lp.release(t, "sd", s_no)
            rlt.acquire(qsy, t)
            qsy.wait("sd", s_no)
            lT = dma(qsy, "lt", lambda e, ri3=ri3: e.dma_start_transpose(
                out=lnt[ri3][:].rearrange("p (hc c) -> p hc c", c=128),
                in_=lnb[ri3][:].rearrange("p (hc c) -> p hc c", c=128)))
            rln.release(t, lT)
            qg.wait(*lT)
            sT = dma(qg, "sp", lambda e, b_=b_, tt=tt, ri3=ri3: e.dma_start(
                out=lnt_s[b_, :, tt * 128:(tt + 1) * 128].rearrange(
                    "(hc p) c -> p hc c", p=128),
                in_=lnt[ri3][:].rearrange("p (hc c) -> p hc c", c=128)))
            rlt.release(t, sT)

        NTOT = B * NT
        for t in range(NTOT):
            aln_load(t)
            if t >= 2:
                aln_norm(t - 2)
        lp.finish()
        aln_norm(NTOT - 2)
        aln_norm(NTOT - 1)
    flush()
    c_aln = snap()

    if "akv" in phases:
        barrier(c_aln)
        rlt, rkv = Ring(2), Ring(2)
        for t in range(B * NT):
            b_, tt = divmod(t, NT)
            ri = t % 2
            rlt.acquire(qsy, t)
            lL = dma(qsy, "lk", lambda e, b_=b_, tt=tt, ri=ri: e.dma_start(
                out=lnt[ri][:].rearrange("p (hc c) -> p hc c", c=128),
                in_=lnt_s[b_, :, tt * 128:(tt + 1) * 128].rearrange(
                    "(hc p) c -> p hc c", p=128)))
            g0 = (t % 2) * 4
            qt_.wait(*lL)
            for bk in range(4):
                bank_wait(qt_, g0 + bk)
            for hc in range(8):
                for ci, col in enumerate((0, 512, 1024, 1536)):
                    last = hc == 7 and ci == 3
                    fn = lambda e, ri=ri, hc=hc, col=col, g0=g0, ci=ci: e.matmul(
                        pb[g0 + ci][:], lnt[ri][:, hc * 128:(hc + 1) * 128],
                        wkv[:, hc * 2048 + col:hc * 2048 + col + 512],
                        start=(hc == 0), stop=(hc == 7))
                    if last:
                        t_kv = cop(qt_, "td", fn)
                    else:
                        qt_(fn)
            rlt.release(t, "td", t_kv)
            rkv.acquire(qsc, t)
            rkv.acquire(qv, t)
            qsc.wait("td", t_kv)
            qv.wait("td", t_kv)
            qsc(lambda e, ri=ri, g0=g0: e.copy(out=kb[ri][:, 0:512], in_=pb[g0][:]))
            s_k = cop(qsc, "sd", lambda e, ri=ri, g0=g0: e.copy(
                out=kb[ri][:, 512:1024], in_=pb[g0 + 1][:]))
            qv(lambda e, ri=ri, g0=g0: e.tensor_copy(out=vbt[ri][:, 0:512], in_=pb[g0 + 2][:]))
            v_v = cop(qv, "vd", lambda e, ri=ri, g0=g0: e.tensor_copy(
                out=vbt[ri][:, 512:1024], in_=pb[g0 + 3][:]))
            bank_set(g0, ("sd", s_k))
            bank_set(g0 + 1, ("sd", s_k))
            bank_set(g0 + 2, ("vd", v_v))
            bank_set(g0 + 3, ("vd", v_v))
            qg.wait("sd", s_k)
            qg.wait("vd", v_v)
            dma(qg, "sp", lambda e, b_=b_, tt=tt, ri=ri: e.dma_start(
                out=kt_s[b_, tt * 32:(tt + 1) * 32, :].rearrange(
                    "d (jh c) -> (d jh) c", jh=4),
                in_=kb[ri][:]))
            sKV = dma(qg, "sp", lambda e, b_=b_, tt=tt, ri=ri: e.dma_start(
                out=v_s[b_, tt], in_=vbt[ri][:]))
            rkv.release(t, sKV)
    flush()
    c_akv = snap()

    if "aq" in phases:
        barrier(c_akv)
        for hc in range(8):
            lW = dma(qsy, "lx", lambda e, hc=hc: e.dma_start(
                out=fe[hc % 2][:], in_=wqt[hc * 128:(hc + 1) * 128, :]))
            qsc.wait(*lW)
            s_w = cop(qsc, "sd", lambda e, hc=hc: e.activation(
                out=wkv[:, hc * 1024:(hc + 1) * 1024], in_=fe[hc % 2][:],
                func=AF.Identity, scale=float(1.0 / 32.0)))
            qsy.wait("sd", s_w)
        flush()
        c_wq = snap()
        barrier({"sd": c_wq["sd"]})
        rx, rln, rlt, rqe = Ring(3), Ring(3), Ring(3), Ring(2)
        lp2 = LNPipe()

        def aq_load(ot):
            ri3 = ot % 3
            rx.acquire(qsy, ot)
            lX = dma(qsy, "lx", lambda e, ot=ot, ri3=ri3: e.dma_start(
                out=fa[ri3][:], in_=xown[ot]))
            qv.wait(*lX)
            lp2.push(ot, fa[ri3][:])

        def aq_norm(ot):
            b_, k_ = divmod(ot, 4)
            ri3 = ot % 3
            rln.acquire(qsc, ot)
            qsc.wait("gd", lp2.gate[ot])
            s_no = cop(qsc, "sd", lambda e, ri3=ri3: e.activation(
                out=lnb[ri3][:], in_=fa[ri3][:], func=AF.Identity,
                scale=rst[ri3][:], bias=nmt[ri3][:]))
            rx.release(ot, "sd", s_no)
            lp2.release(ot, "sd", s_no)
            rlt.acquire(qsy, ot)
            qsy.wait("sd", s_no)
            lT = dma(qsy, "lt", lambda e, ri3=ri3: e.dma_start_transpose(
                out=lnt[ri3][:].rearrange("p (hc c) -> p hc c", c=128),
                in_=lnb[ri3][:].rearrange("p (hc c) -> p hc c", c=128)))
            rln.release(ot, lT)
            qt_.wait(*lT)
            for ds in range(8):
                bank = ds % 2
                bank_wait(qt_, bank)
                for hc in range(8):
                    fn = lambda e, ri3=ri3, ds=ds, hc=hc, bank=bank: e.matmul(
                        pb[bank][:, 0:128],
                        wkv[:, hc * 1024 + ds * 128:hc * 1024 + (ds + 1) * 128],
                        lnt[ri3][:, hc * 128:(hc + 1) * 128],
                        start=(hc == 0), stop=(hc == 7))
                    if hc == 7:
                        t_q = cop(qt_, "td", fn)
                    else:
                        qt_(fn)
                if ds == 7:
                    rlt.release(ot, "td", t_q)
                qi = ds % 2
                rqe.acquire(qv, ot * 8 + ds)
                qv.wait("td", t_q)
                v_e = cop(qv, "vd", lambda e, qi=qi, bank=bank: e.tensor_copy(
                    out=qe[qi][:, 0:128], in_=pb[bank][:, 0:128]))
                bank_set(bank, ("vd", v_e))
                qg.wait("vd", v_e)
                s_qe = dma(qg, "sp", lambda e, qi=qi, ds=ds, b_=b_, k_=k_: e.dma_start(
                    out=qt_s[ds, :, b_, k_ * 128:(k_ + 1) * 128],
                    in_=qe[qi][:, 0:128]))
                rqe.release(ot * 8 + ds, s_qe)

        for ot in range(32):
            aq_load(ot)
            if ot >= 2:
                aq_norm(ot - 2)
        lp2.finish()
        aq_norm(30)
        aq_norm(31)
    flush()
    c_aq = snap()

    if "avs" in phases:
        barrier(c_aq)
        rvb = Ring(2)
        for b_ in range(B):
            ti0 = b_ * 24
            n_ = 0
            for jt in range(8, 32):
                ti = ti0 + n_
                n_ += 1
                ri = ti % 2
                rvb.acquire(qg, ti)
                lV = dma(qg, "lv", lambda e, b_=b_, jt=jt, ri=ri: e.dma_start(
                    out=vbt[ri][:], in_=v_s[b_, jt]))
                qt_.wait(*lV)
                o = jt // 8
                fns = []
                for k_ in range(o):
                    for hf in range(2):
                        first = (jt == 8 * (k_ + 1))
                        last = (jt == 31)
                        fns.append(lambda e, ri=ri, k_=k_, hf=hf, first=first,
                                   last=last:
                                   e.matmul(pb[2 * k_ + hf][0:1, :],
                                            onec[:],
                                            vbt[ri][:, hf * 512:(hf + 1) * 512],
                                            start=first, stop=last))
                for fn in fns[:-1]:
                    qt_(fn)
                t_l = cop(qt_, "td", fns[-1])
                rvb.release(ti, "td", t_l)
            qv.wait("td", cnt["td"])
            v_e = None
            for k_ in range(3):
                qv(lambda e, k_=k_: e.tensor_copy(
                    out=sufst[k_][0:1, 0:512], in_=pb[2 * k_][0:1, :]))
                v_e = cop(qv, "vd", lambda e, k_=k_: e.tensor_copy(
                    out=sufst[k_][0:1, 512:1024], in_=pb[2 * k_ + 1][0:1, :]))
                bank_set(2 * k_, ("vd", v_e))
                bank_set(2 * k_ + 1, ("vd", v_e))
            qg.wait("vd", v_e)
            for k_ in range(3):
                g_s = cop(qg, "gd", lambda e, k_=k_: e.tensor_scalar(
                    out=sufst[3][0:1, :], in0=sufst[k_][0:1, :],
                    scalar1=E_NEG9, scalar2=None, op0=ALU.mult))
                qsy.wait("gd", g_s)
                sS = dma(qsy, "sp", lambda e, b_=b_, k_=k_: e.dma_start(
                    out=suf_h[b_, k_].rearrange("c -> () c"), in_=sufst[3][0:1, :]))
                qg.wait(*sS)
    flush()
    c_avs = snap()

    if "b1a" in phases:
        barrier(c_avs)
        rkt = Ring(2)
        rex = Ring(3)
        rqtb = Ring(1)
        for b_ in range(B):
            qb = 0
            rqtb.acquire(qsy, b_)
            lQ = dma(qsy, "lx", lambda e, b_=b_, qb=qb: e.dma_start(
                out=qtb[qb][:].rearrange("p (d i) -> p d i", i=512),
                in_=qt_s[:, :, b_, :].rearrange("d p i -> p d i")))
            if qtb_d is not None:
                qsy.wait(*lQ)
                dma(qsy, "sp", lambda e, b_=b_, qb=qb: e.dma_start(
                    out=qtb_d[b_], in_=qtb[qb][:]))
            qt_.wait(*lQ)
            for jt in range(32):
                ti = b_ * 32 + jt
                k0 = jt // 8
                w = 512 - 128 * k0
                off = 128 * k0
                ri = ti % 2
                rkt.acquire(qsy, ti)
                lK = dma(qsy, "lk", lambda e, b_=b_, jt=jt, ri=ri: e.dma_start(
                    out=kb[ri][:].rearrange("p (d j) -> p d j", j=128),
                    in_=kt_s[b_, :, jt * 128:(jt + 1) * 128].rearrange(
                        "(d p) j -> p d j", p=128)))
                bank = ti % 2
                qt_.wait(*lK)
                bank_wait(qt_, bank)
                for dc in range(8):
                    fn = lambda e, ri=ri, dc=dc, bank=bank, w=w, off=off, qb=qb: \
                        e.matmul(
                        pb[bank][:, 0:w], kb[ri][:, dc * 128:(dc + 1) * 128],
                        qtb[qb][:, dc * 512 + off:dc * 512 + off + w],
                        start=(dc == 0), stop=(dc == 7))
                    if dc == 7:
                        t_s = cop(qt_, "td", fn)
                    else:
                        qt_(fn)
                rkt.release(ti, "td", t_s)
                if jt == 31:
                    rqtb.release(b_, "td", t_s)
                ei = ti % 3
                rex.acquire(qsc, ti)
                qsc.wait("td", t_s)
                s_e = cop(qsc, "sd", lambda e, ei=ei, bank=bank, w=w: e.activation(
                    out=fa[ei][:, 0:w], in_=pb[bank][:, 0:w], func=AF.Exp))
                bank_set(bank, ("sd", s_e))
                qg.wait("sd", s_e)
                sE = dma(qg, "sp", lambda e, b_=b_, jt=jt, ei=ei, w=w: e.dma_start(
                    out=e_s[b_, jt][:, 0:w], in_=fa[ei][:, 0:w]))
                rex.release(ti, sE)
    flush()
    c_b1a = snap()

    if "b1b" in phases:
        barrier(c_b1a)
        SLOT = {
            0: dict(e=[(ew[0], 512 * b_) for b_ in range(8)],
                    l1=[(fd[1], 0), (fd[1], 512), (fe[0], 0), (fe[0], 512)],
                    l2=[(fa[1], 0), (fa[1], 512)],
                    S=(fb[0], 0), R=(srt, 0)),
            1: dict(e=[(ew[0], 512 * b_) for b_ in range(8)],
                    l1=[(fe[1], 0), (fe[1], 512), (fa[0], 0), (fa[0], 512)],
                    l2=[(fa[2], 0), (fa[2], 512)],
                    S=(fb[0], 512), R=(srt, 512)),
        }
        relo = Ring(1)
        rms = Ring(2)
        rpt = Ring(2)
        for jt in range(32):
            k0 = jt // 8
            w = 512 - 128 * k0
            off = 128 * k0
            g = jt % 2
            sl = SLOT[g]
            rms.acquire(qsy, jt)
            dma(qsy, "lk", lambda e, jt=jt, g=g, off=off, w=w: e.dma_start(
                out=mmb[g][:, 0:w], in_=mmul[jt][:, off:off + w]))
            lM = dma(qsy, "lk", lambda e, jt=jt, g=g, off=off, w=w: e.dma_start(
                out=mab[g][:, 0:w], in_=madd[jt][:, off:off + w]))
            relo.acquire(qsy, jt)
            lE = dma(qsy, "le", lambda e, jt=jt, w=w: e.dma_start(
                out=ew[0][:].rearrange("p (b i) -> p b i", i=512)[:, :, 0:w],
                in_=e_s[:, jt, :, 0:w].rearrange("b p i -> p b i")))
            qv.wait(*lE)
            v_l1 = None
            for i4 in range(4):
                t0, c0 = sl["e"][2 * i4]
                t1, c1 = sl["e"][2 * i4 + 1]
                ta, ca = sl["l1"][i4]
                fn = lambda e, t0=t0, c0=c0, t1=t1, c1=c1, ta=ta, ca=ca, w=w: \
                    e.tensor_tensor(out=ta[:, ca:ca + w], in0=t0[:, c0:c0 + w],
                                    in1=t1[:, c1:c1 + w], op=ALU.add)
                if i4 == 3:
                    v_l1 = cop(qv, "vd", fn)
                else:
                    qv(fn)
            qg.wait("vd", v_l1)
            for i2 in range(2):
                p0, q0 = sl["l1"][2 * i2]
                p1, q1 = sl["l1"][2 * i2 + 1]
                ta, ca = sl["l2"][i2]
                fn = lambda e, p0=p0, q0=q0, p1=p1, q1=q1, ta=ta, ca=ca, w=w: \
                    e.tensor_tensor(out=ta[:, ca:ca + w], in0=p0[:, q0:q0 + w],
                                    in1=p1[:, q1:q1 + w], op=ALU.add)
                if i2 == 1:
                    g_l2 = cop(qg, "gd", fn)
                else:
                    qg(fn)
            qv.wait("gd", g_l2)
            St, Sc = sl["S"]
            qv(lambda e, a=sl["l2"][0], b2_=sl["l2"][1], St=St, Sc=Sc, w=w:
               e.tensor_tensor(out=St[:, Sc:Sc + w], in0=a[0][:, a[1]:a[1] + w],
                               in1=b2_[0][:, b2_[1]:b2_[1] + w], op=ALU.add))
            Rt, Rc = sl["R"]
            qv(lambda e, St=St, Sc=Sc, Rt=Rt, Rc=Rc, w=w:
               e.reciprocal(out=Rt[:, Rc:Rc + w], in_=St[:, Sc:Sc + w]))
            qv.wait(*lM)
            qv(lambda e, Rt=Rt, Rc=Rc, St=St, Sc=Sc, g=g, w=w:
               e.tensor_tensor(out=St[:, Sc:Sc + w], in0=Rt[:, Rc:Rc + w],
                               in1=mmb[g][:, 0:w], op=ALU.mult))
            v_last = None
            for b_ in range(B):
                t_, c_ = sl["e"][b_]
                tt_, tc_ = sl["l1"][b_ % 2]
                qv(lambda e, t_=t_, c_=c_, tt_=tt_, tc_=tc_, St=St, Sc=Sc, w=w:
                   e.tensor_tensor(out=tt_[:, tc_:tc_ + w], in0=t_[:, c_:c_ + w],
                                   in1=St[:, Sc:Sc + w], op=ALU.mult))
                rpt.acquire(qv, jt * 8 + b_)
                v_last = cop(qv, "vd", lambda e, tt_=tt_, tc_=tc_, g=g, b_=b_, w=w:
                             e.tensor_tensor(out=ptb[b_ % 2][:, 0:w],
                                             in0=tt_[:, tc_:tc_ + w],
                                             in1=mab[g][:, 0:w], op=ALU.add))
                qg.wait("vd", v_last)
                sP = dma(qg, "sp", lambda e, jt=jt, b_=b_, off=off, w=w: e.dma_start(
                    out=pt_s[jt, b_][:, off:off + w], in_=ptb[b_ % 2][:, 0:w]))
                rpt.release(jt * 8 + b_, sP)
            relo.release(jt, "vd", v_last)
            rms.release(jt, "vd", v_last)
    flush()
    c_b1b = snap()

    def ln_simple(X_ap, ri, spacer_to):
        qv(lambda e: e.bn_stats(out=st6[ri][:, 0:6], in_=X_ap[:, 0:512]))
        qv(lambda e: e.bn_stats(out=st6[ri][:, 6:12], in_=X_ap[:, 512:1024]))
        qv(lambda e: e.tensor_copy(out=spacer_to[:], in_=X_ap))
        v_ag = cop(qv, "vd", lambda e: e.bn_aggr(out=mvt[ri][:], in_=st6[ri][:]))
        qg.wait("vd", v_ag)
        g_e = cop(qg, "gd", lambda e: e.tensor_scalar(
            out=vet[ri][:], in0=mvt[ri][:, 1:2], scalar1=EPS, scalar2=None,
            op0=ALU.add))
        qsc.wait("gd", g_e)
        s_sq = cop(qsc, "sd", lambda e: e.activation(
            out=sqt[ri][:], in_=vet[ri][:], func=AF.Sqrt))
        qv.wait("sd", s_sq)
        v_rc = cop(qv, "vd", lambda e: e.reciprocal(out=rst[ri][:], in_=sqt[ri][:]))
        qg.wait("vd", v_rc)
        g_nm = cop(qg, "gd", lambda e: e.tensor_scalar(
            out=nmt[ri][:], in0=mvt[ri][:, 0:1], scalar1=rst[ri][:], scalar2=-1.0,
            op0=ALU.mult, op1=ALU.mult))
        return g_nm

    if "b2" in phases:
        barrier(c_b1b)
        rv2 = Ring(2)
        rpt2 = Ring(2)
        rev = Ring(4)
        rsuf = Ring(3)
        evslots = [(fb[0], 0), (fb[0], 512), (fb[1], 0), (fb[1], 512)]
        nev = 0
        for pr in range(4):
            b0, b1 = 2 * pr, 2 * pr + 1
            for hf in range(2):
                suf_toks = []
                for k_ in range(3):
                    rsuf.acquire(qsy, k_)
                    suf_toks.append(dma(qsy, "lx", lambda e, k_=k_, b0=b0, hf=hf:
                                        e.dma_start(
                        out=sufst[k_][0:1, :].rearrange("o (b c) -> o b c", c=512),
                        in_=suf_h[b0:b0 + 2, k_,
                                  hf * 512:(hf + 1) * 512].rearrange(
                            "b c -> () b c"))))
                for jt in range(32):
                    k0 = jt // 8
                    off = 128 * k0
                    w = 512 - off
                    g = jt % 2
                    rv2.acquire(qg, jt)
                    dma(qg, "lv", lambda e, b0=b0, jt=jt, hf=hf, g=g: e.dma_start(
                        out=vbt[g][:, 0:512], in_=v_s[b0, jt][:, hf * 512:(hf + 1) * 512]))
                    lV = dma(qg, "lv", lambda e, b1=b1, jt=jt, hf=hf, g=g: e.dma_start(
                        out=vbt[g][:, 512:1024], in_=v_s[b1, jt][:, hf * 512:(hf + 1) * 512]))
                    rpt2.acquire(qsy, jt)
                    lP = dma(qsy, "le", lambda e, jt=jt, b0=b0, g=g, off=off, w=w:
                             e.dma_start(
                        out=ptin[g][:, 0:2 * w].rearrange("p (b i) -> p b i", i=w),
                        in_=pt_s[jt, b0:b0 + 2, :, off:off + w].rearrange(
                            "b p i -> p b i")))
                    qt_.wait(*lV)
                    qt_.wait(*lP)
                    if jt % 8 == 7:
                        for tk_ in suf_toks:
                            qt_.wait(*tk_)
                    fns = []
                    for bi in range(2):
                        for k_ in range(k0, 4):
                            bank = bi * 4 + k_
                            if jt == 0:
                                bank_wait(qt_, bank)
                            lo = bi * w + (k_ * 128 - off)
                            stop = (k_ == 3 and jt == 31)
                            fns.append(lambda e, g=g, bi=bi, bank=bank, lo=lo,
                                       first=(jt == 0), stop=stop: e.matmul(
                                pb[bank][:], ptin[g][:, lo:lo + 128],
                                vbt[g][:, bi * 512:(bi + 1) * 512],
                                start=first, stop=stop))
                    kc = jt // 8
                    if jt % 8 == 7:
                        for bi in range(2):
                            bank = bi * 4 + kc
                            if kc < 3:
                                fns.append(lambda e, bank=bank, kc=kc, bi=bi:
                                           e.matmul(
                                    pb[bank][:], ones1[:],
                                    sufst[kc][0:1, bi * 512:(bi + 1) * 512],
                                    start=False, stop=True))
                    for fn in fns[:-1]:
                        qt_(fn)
                    t_j = cop(qt_, "td", fns[-1])
                    rv2.release(jt, "td", t_j)
                    rpt2.release(jt, "td", t_j)
                    if jt % 8 == 7:
                        for bi, b_ in enumerate((b0, b1)):
                            bank = bi * 4 + kc
                            ot = b_ * 4 + kc
                            sl_, co_ = evslots[nev % 4]
                            rev.acquire(qv, nev)
                            qv.wait("td", t_j)
                            v_ev = cop(qv, "vd", lambda e, bank=bank, sl_=sl_,
                                       co_=co_: e.tensor_copy(
                                out=sl_[:, co_:co_ + 512], in_=pb[bank][:]))
                            bank_set(bank, ("vd", v_ev))
                            if kc < 3:
                                rsuf.release(kc, "vd", v_ev)
                            qg.wait("vd", v_ev)
                            sA = dma(qg, "sp", lambda e, ot=ot, hf=hf, sl_=sl_,
                                     co_=co_: e.dma_start(
                                out=att_h[ot][:, hf * 512:(hf + 1) * 512],
                                in_=sl_[:, co_:co_ + 512]))
                            rev.release(nev, sA)
                            nev += 1
    flush()
    c_b2 = snap()

    if "b2e" in phases:
        barrier(c_b2)
        rat = Ring(2)
        rxo = Ring(2)
        rrs = Ring(2)
        for ot in range(32):
            ri = ot % 2
            rat.acquire(qsy, ot)
            lA = dma(qsy, "lx", lambda e, ot=ot, ri=ri: e.dma_start(
                out=fa[ri][:], in_=att_h[ot]))
            rxo.acquire(qsy, ot)
            lX = dma(qsy, "lx", lambda e, ot=ot, ri=ri: e.dma_start(
                out=fcx[ri][:], in_=xown[ot]))
            qv.wait(*lA)
            g_nm = ln_simple(fa[ri][:], ri, fe[ri])
            qsc.wait("gd", g_nm)
            s_no = cop(qsc, "sd", lambda e, ri=ri: e.activation(
                out=fd[ri][:], in_=fa[ri][:], func=AF.Identity,
                scale=rst[ri][:], bias=nmt[ri][:]))
            rat.release(ot, "sd", s_no)
            qv.wait("sd", s_no)
            qv.wait(*lX)
            rrs.acquire(qv, ot)
            v_x2 = cop(qv, "vd", lambda e, ri=ri: e.tensor_tensor(
                out=fb[ri][:], in0=fd[ri][:], in1=fcx[ri][:], op=ALU.add))
            rxo.release(ot, "vd", v_x2)
            qg.wait("vd", v_x2)
            sX2 = dma(qg, "sp", lambda e, ot=ot, ri=ri: e.dma_start(
                out=x2_s[ot], in_=fb[ri][:]))
            rrs.release(ot, sX2)
    flush()
    c_b2e = snap()

    if "cw" in phases:
        barrier(c_b2e)
        rw = Ring(3)
        for i_ in range(32):
            dc, fq = divmod(i_, 4)
            ri = i_ % 3
            rw.acquire(qsy, i_)
            lW = dma(qsy, "lx", lambda e, dc=dc, fq=fq, ri=ri: e.dma_start(
                out=fa[ri][:], in_=w1t[dc * 128:(dc + 1) * 128,
                                      fq * 1024:(fq + 1) * 1024]))
            qsc.wait(*lW)
            s_w8 = cop(qsc, "sd", lambda e, ri=ri, i_=i_: e.activation(
                out=xq8[i_ % 2][:], in_=fa[ri][:], func=AF.Identity))
            qv.wait(*lW)
            qv(lambda e, ri=ri: e.tensor_scalar(
                out=fb[0][:], in0=fa[ri][:], scalar1=C_Q4, scalar2=G_Q4,
                op0=ALU.mult, op1=ALU.max))
            qv(lambda e, ri=ri: e.tensor_scalar(
                out=fb[1][:], in0=fa[ri][:], scalar1=-C_Q4, scalar2=G_Q4,
                op0=ALU.mult, op1=ALU.max))
            qv(lambda e: e.tensor_tensor(
                out=fcx[0][:], in0=fb[0][:], in1=fb[1][:], op=ALU.max))
            qv(lambda e, ri=ri: e.tensor_tensor(
                out=fcx[1][:], in0=fa[ri][:], in1=fcx[0][:], op=ALU.add))
            v_c = cop(qv, "vd", lambda e: e.tensor_tensor(
                out=fd[0][:], in0=fcx[1][:], in1=fcx[0][:], op=ALU.subtract))
            qsc.wait("vd", v_c)
            s_z = cop(qsc, "sd", lambda e, i_=i_: e.activation(
                out=fd[1][:], in_=xq8[i_ % 2][:], func=AF.Identity,
                scale=z32t[:]))
            qv.wait("sd", s_z)
            v_f = cop(qv, "vd", lambda e, i_=i_: e.tensor_tensor(
                out=xq8[i_ % 2][:], in0=fd[1][:], in1=fd[0][:], op=ALU.add))
            rw.release(i_, "vd", v_f)
            qg.wait("vd", v_f)
            sW1 = dma(qg, "sp", lambda e, dc=dc, fq=fq, i_=i_: e.dma_start(
                out=w1q_s[dc, :, fq * 1024:(fq + 1) * 1024], in_=xq8[i_ % 2][:]))
            rw.release(i_, sW1)
        rkb = Ring(2)
        for fc in range(32):
            ri = fc % 3
            rw.acquire(qsy, 32 + fc)
            lW = dma(qsy, "lx", lambda e, fc=fc, ri=ri: e.dma_start(
                out=fa[ri][:], in_=w2t[fc * 128:(fc + 1) * 128, :]))
            qsc.wait(*lW)
            s_w8 = cop(qsc, "sd", lambda e, ri=ri, fc=fc: e.activation(
                out=y8t[fc % 2][:], in_=fa[ri][:], func=AF.Identity))
            qv.wait(*lW)
            qv(lambda e, ri=ri: e.tensor_scalar(
                out=fb[0][:], in0=fa[ri][:], scalar1=C_Q4, scalar2=G_Q4,
                op0=ALU.mult, op1=ALU.max))
            qv(lambda e, ri=ri: e.tensor_scalar(
                out=fb[1][:], in0=fa[ri][:], scalar1=-C_Q4, scalar2=G_Q4,
                op0=ALU.mult, op1=ALU.max))
            qv(lambda e: e.tensor_tensor(
                out=fcx[0][:], in0=fb[0][:], in1=fb[1][:], op=ALU.max))
            qv(lambda e, ri=ri: e.tensor_tensor(
                out=fcx[1][:], in0=fa[ri][:], in1=fcx[0][:], op=ALU.add))
            v_c = cop(qv, "vd", lambda e: e.tensor_tensor(
                out=fd[0][:], in0=fcx[1][:], in1=fcx[0][:], op=ALU.subtract))
            qsc.wait("vd", v_c)
            s_z = cop(qsc, "sd", lambda e, fc=fc: e.activation(
                out=fd[1][:], in_=y8t[fc % 2][:], func=AF.Identity,
                scale=z32t[:]))
            qv.wait("sd", s_z)
            rkb.acquire(qv, fc)
            v_f = cop(qv, "vd", lambda e, fc=fc: e.tensor_tensor(
                out=kb[fc % 2][:], in0=fd[1][:], in1=fd[0][:], op=ALU.add))
            rw.release(32 + fc, "vd", v_f)
            qg.wait("vd", v_f)
            sW2 = dma(qg, "sp", lambda e, fc=fc: e.dma_start(
                out=w2q_s[:, :, fc, :].rearrange("d p c -> p d c"),
                in_=kb[fc % 2][:].rearrange("p (d c) -> p d c", c=128)))
            rkb.release(fc, sW2)
    flush()
    c_cw = snap()

    if "c" in phases or "c1" in phases or "c12" in phases:
        do_c2 = "c" in phases or "c12" in phases or "c3a" in phases or "c3b" in phases
        do_c3 = "c" in phases or "c3a" in phases or "c3b" in phases
        do_tail = "c" in phases or "c3b" in phases
        do_ln3 = "c" in phases
        barrier(c_cw)
        XQT = [lnb[0], lnb[1], lnb[2], lnt[0], lnt[1], lnt[2], kb[0], kb[1]]
        c_prev = c_cw
        for ch in range(4):
            rxc = Ring(2)
            rxq = Ring(2)
            rxq2 = Ring(2)
            for j in range(8):
                ot = ch * 8 + j
                ri = j % 2
                rxc.acquire(qsy, j)
                lX = dma(qsy, "lx", lambda e, ot=ot, ri=ri: e.dma_start(
                    out=fa[ri][:], in_=x2_s[ot]))
                qv.wait(*lX)
                g_nm = ln_simple(fa[ri][:], ri, fe[ri])
                qsc.wait("gd", g_nm)
                qsc(lambda e, ri=ri: e.activation(
                    out=xq8[ri][:], in_=fa[ri][:], func=AF.Identity,
                    scale=rst[ri][:], bias=nmt[ri][:]))
                s_ln = cop(qsc, "sd", lambda e, ri=ri: e.activation(
                    out=fb[ri][:], in_=fa[ri][:], func=AF.Identity,
                    scale=rst[ri][:], bias=nmt[ri][:]))
                rxc.release(j, "sd", s_ln)
                qv.wait("sd", s_ln)
                qv(lambda e, ri=ri: e.tensor_scalar(
                    out=fcx[0][:], in0=fb[ri][:], scalar1=C_Q4, scalar2=G_Q4,
                    op0=ALU.mult, op1=ALU.max))
                qv(lambda e, ri=ri: e.tensor_scalar(
                    out=fcx[1][:], in0=fb[ri][:], scalar1=-C_Q4, scalar2=G_Q4,
                    op0=ALU.mult, op1=ALU.max))
                qv(lambda e: e.tensor_tensor(
                    out=fd[0][:], in0=fcx[0][:], in1=fcx[1][:], op=ALU.max))
                qv(lambda e, ri=ri: e.tensor_tensor(
                    out=fd[1][:], in0=fb[ri][:], in1=fd[0][:], op=ALU.add))
                v_c4 = cop(qv, "vd", lambda e: e.tensor_tensor(
                    out=fe[1 - ri][:] if False else fcx[0][:],
                    in0=fd[1][:], in1=fd[0][:], op=ALU.subtract))
                qsc.wait("vd", v_c4)
                s_z = cop(qsc, "sd", lambda e, ri=ri: e.activation(
                    out=fcx[1][:], in_=xq8[ri][:], func=AF.Identity,
                    scale=z32t[:]))
                qv.wait("sd", s_z)
                rxq.acquire(qv, j)
                rxq2.acquire(qv, j)
                v_xq = cop(qv, "vd", lambda e, ri=ri: e.tensor_tensor(
                    out=vbt[ri][:], in0=fcx[1][:], in1=fcx[0][:], op=ALU.add))
                qsy.wait("vd", v_xq)
                ltoks = []
                for dc in range(8):
                    ltoks.append(dma(qsy, "lt", lambda e, ri=ri, dc=dc, j=j:
                                     e.dma_start_transpose(
                        out=XQT[dc][:, j * 128:(j + 1) * 128],
                        in_=vbt[ri][:, dc * 128:(dc + 1) * 128])))
                rxq.release(j, ltoks[-2])
                rxq2.release(j, ltoks[-1])
            flush()
            c_c1 = snap()
            barrier(c_c1)
            rw1 = Ring(2)
            for fs in range(32 if do_c2 else 0):
                pg = 4 + 2 * (fs % 2)
                w1o = 32768 + (fs % 2) * 1024
                rw1.acquire(qsy, fs)
                lw1 = dma(qsy, "lk", lambda e, fs=fs, w1o=w1o: e.dma_start(
                    out=f8a[:, w1o:w1o + 1024].rearrange("p (d c) -> p d c", c=128),
                    in_=w1q_s[:, :, fs * 128:(fs + 1) * 128].rearrange(
                        "d p c -> p d c")))
                bank_wait(qt_, pg)
                bank_wait(qt_, pg + 1)
                qt_.wait(*lw1)
                fns = []
                for dc in range(8):
                    for hf2 in range(2):
                        fns.append(lambda e, dc=dc, hf2=hf2, pg=pg, w1o=w1o:
                                   e.matmul(
                            pb[pg + hf2][:],
                            f8a[:, w1o + dc * 128:w1o + (dc + 1) * 128],
                            XQT[dc][:, hf2 * 512:(hf2 + 1) * 512],
                            start=(dc == 0), stop=(dc == 7)))
                for fn in fns[:-1]:
                    qt_(fn)
                t_h = cop(qt_, "td", fns[-1])
                rw1.release(fs, "td", t_h)
                qsc.wait("td", t_h)
                qsc(lambda e, pg=pg, fs=fs: e.activation(
                    out=h5t[fs % 2][:, 0:512], in_=pb[pg][:], func=AF.Identity,
                    bias=b1q[:, fs:fs + 1]))
                s_h5 = cop(qsc, "sd", lambda e, pg=pg, fs=fs: e.activation(
                    out=h5t[fs % 2][:, 512:1024], in_=pb[pg + 1][:],
                    func=AF.Identity, bias=b1q[:, fs:fs + 1]))
                qv.wait("td", t_h)
                qv.wait("sd", s_h5)
                qv(lambda e, pg=pg, fs=fs: e.tensor_scalar(
                    out=fb[0][:, 0:512], in0=pb[pg][:], scalar1=b1q[:, fs:fs + 1],
                    scalar2=None, op0=ALU.add))
                v_hf = cop(qv, "vd", lambda e, pg=pg, fs=fs: e.tensor_scalar(
                    out=fb[0][:, 512:1024], in0=pb[pg + 1][:],
                    scalar1=b1q[:, fs:fs + 1], scalar2=None, op0=ALU.add))
                bank_set(pg, ("vd", v_hf), ("sd", s_h5))
                bank_set(pg + 1, ("vd", v_hf), ("sd", s_h5))
                qv(lambda e: e.tensor_scalar(
                    out=fcx[0][:], in0=fb[0][:], scalar1=C_Q5, scalar2=G_Q5,
                    op0=ALU.mult, op1=ALU.max))
                qv(lambda e: e.tensor_scalar(
                    out=fcx[1][:], in0=fb[0][:], scalar1=-C_Q5, scalar2=G_Q5,
                    op0=ALU.mult, op1=ALU.max))
                qv(lambda e: e.tensor_tensor(
                    out=fd[0][:], in0=fcx[0][:], in1=fcx[1][:], op=ALU.max))
                qv(lambda e: e.tensor_tensor(
                    out=fd[1][:], in0=fb[0][:], in1=fd[0][:], op=ALU.add))
                v_c5 = cop(qv, "vd", lambda e: e.tensor_tensor(
                    out=fe[0][:], in0=fd[1][:], in1=fd[0][:], op=ALU.subtract))
                qsc.wait("vd", v_c5)
                s_z5 = cop(qsc, "sd", lambda e, fs=fs: e.activation(
                    out=fe[1][:], in_=h5t[fs % 2][:], func=AF.Identity,
                    scale=z32t[:]))
                qv.wait("sd", s_z5)
                v_h5q = cop(qv, "vd", lambda e: e.tensor_tensor(
                    out=fb[1][:], in0=fe[1][:], in1=fe[0][:], op=ALU.add))
                qsc.wait("vd", v_h5q)
                s_y8 = cop(qsc, "sd", lambda e, fs=fs: e.activation(
                    out=y8t[fs % 2][:], in_=fb[1][:], func=AF.Relu))
                qv(lambda e: e.tensor_scalar(
                    out=fcx[0][:], in0=fb[1][:], scalar1=0.0, scalar2=None,
                    op0=ALU.max))
                qv(lambda e: e.tensor_scalar(
                    out=fcx[1][:], in0=fcx[0][:], scalar1=C_Q4, scalar2=G_Q4,
                    op0=ALU.mult, op1=ALU.max))
                qv(lambda e: e.tensor_tensor(
                    out=fd[0][:], in0=fcx[0][:], in1=fcx[1][:], op=ALU.add))
                v_c4h = cop(qv, "vd", lambda e: e.tensor_tensor(
                    out=fd[1][:], in0=fd[0][:], in1=fcx[1][:], op=ALU.subtract))
                qsc.wait("vd", v_c4h)
                s_z4 = cop(qsc, "sd", lambda e, fs=fs: e.activation(
                    out=fe[0][:], in_=y8t[fs % 2][:], func=AF.Identity,
                    scale=z32t[:]))
                qv.wait("sd", s_z4)
                v_hq = cop(qv, "vd", lambda e, fs=fs: e.tensor_tensor(
                    out=f8a[:, fs * 1024:(fs + 1) * 1024],
                    in0=fe[0][:], in1=fd[1][:], op=ALU.add))
            flush()
            c_c2 = snap()
            barrier(c_c2)
            rw2 = Ring(2)
            rdr = Ring(2)
            for ts2 in range(2 if do_c3 else 0):
                qv.wait("so0", cnt["so0"])
                for dout in range(8):
                    g2 = dout % 2
                    rw2.acquire(qsy, ts2 * 8 + dout)
                    lw2 = dma(qsy, "lk", lambda e, dout=dout, g2=g2: e.dma_start(
                        out=wkv[:, g2 * 8192:g2 * 8192 + 4096],
                        in_=w2q_s[dout].rearrange("p f c -> p (f c)")))
                    _c3e0 = 2
                    rdr.acquire(qsy, ts2 * 8 + dout)
                    if _c3e0 >= 0:
                        dma(qsy, "lx", lambda e, dout=dout, g2=g2, ch=ch, ts2=ts2:
                            e.dma_start(
                            out=mmb[g2][:],
                            in_=d32[dout * 128:(dout + 1) * 128,
                                    ch * 1024 + ts2 * 512:ch * 1024 + ts2 * 512 + 512]))
                        lDr = dma(qsy, "lx", lambda e, dout=dout, g2=g2, ch=ch, ts2=ts2:
                                  e.dma_start(
                            out=mab[g2][:],
                            in_=d64[dout * 128:(dout + 1) * 128,
                                    ch * 1024 + ts2 * 512:ch * 1024 + ts2 * 512 + 512]))
                    qt_.wait(*lw2)
                    bank_wait(qt_, g2)
                    fns = []
                    for fc in range(32):
                        fns.append(lambda e, fc=fc, g2=g2, ts2=ts2: e.matmul(
                            pb[g2][:],
                            wkv[:, g2 * 8192 + fc * 128:g2 * 8192 + (fc + 1) * 128],
                            f8a[:, fc * 1024 + ts2 * 512:
                                fc * 1024 + ts2 * 512 + 512],
                            start=(fc == 0), stop=(fc == 31)))
                    for fn in fns[:-1]:
                        qt_(fn)
                    t_y = cop(qt_, "td", fns[-1])
                    rw2.release(ts2 * 8 + dout, "td", t_y)
                    _c3e = 2
                    if _c3e0 <= 0:
                        qv.wait("td", t_y)
                        v_mn = cop(qv, "vd", lambda e, g2=g2: e.tensor_copy(
                            out=fb[0][:, 0:512], in_=pb[g2][:]))
                        bank_set(g2, ("vd", v_mn))
                        v_yd = cop(qv, "vd", lambda e, dout=dout: e.tensor_copy(
                            out=ew[0][:, dout * 512:(dout + 1) * 512],
                            in_=fb[0][:, 0:512]))
                        rdr.release(ts2 * 8 + dout, "vd", v_yd)
                        continue
                    if _c3e0 == 1:
                        qsc.wait("td", t_y)
                        s_y5 = cop(qsc, "sd", lambda e, g2=g2, dout=dout: e.activation(
                            out=y5t[g2][:], in_=pb[g2][:], func=AF.Identity,
                            bias=b2q[:, dout:dout + 1]))
                        qv.wait("td", t_y)
                        qv.wait("sd", s_y5)
                        v_yf = cop(qv, "vd", lambda e, g2=g2, dout=dout: e.tensor_scalar(
                            out=fb[0][:, 0:512], in0=pb[g2][:],
                            scalar1=b2q[:, dout:dout + 1], scalar2=None, op0=ALU.add))
                        bank_set(g2, ("vd", v_yf), ("sd", s_y5))
                        v_yd = cop(qv, "vd", lambda e, dout=dout: e.tensor_copy(
                            out=ew[0][:, dout * 512:(dout + 1) * 512],
                            in_=fb[0][:, 0:512]))
                        rdr.release(ts2 * 8 + dout, "vd", v_yd)
                        continue
                    qsc.wait("td", t_y)
                    s_y5 = cop(qsc, "sd", lambda e, g2=g2, dout=dout: e.activation(
                        out=y5t[g2][:], in_=pb[g2][:], func=AF.Identity,
                        bias=b2q[:, dout:dout + 1]))
                    qv.wait("td", t_y)
                    qv.wait("sd", s_y5)
                    v_yf = cop(qv, "vd", lambda e, g2=g2, dout=dout: e.tensor_scalar(
                        out=fb[0][:, 0:512], in0=pb[g2][:],
                        scalar1=b2q[:, dout:dout + 1], scalar2=None, op0=ALU.add))
                    bank_set(g2, ("vd", v_yf), ("sd", s_y5))
                    if _c3e == 0:
                        v_yd = cop(qv, "vd", lambda e, dout=dout: e.tensor_copy(
                            out=ew[0][:, dout * 512:(dout + 1) * 512],
                            in_=fb[0][:, 0:512]))
                        rdr.release(ts2 * 8 + dout, "vd", v_yd)
                        continue
                    qv(lambda e: e.tensor_scalar(
                        out=fcx[0][:, 0:512], in0=fb[0][:, 0:512], scalar1=C_Q5,
                        scalar2=G_Q5, op0=ALU.mult, op1=ALU.max))
                    qv(lambda e: e.tensor_scalar(
                        out=fcx[1][:, 0:512], in0=fb[0][:, 0:512], scalar1=-C_Q5,
                        scalar2=G_Q5, op0=ALU.mult, op1=ALU.max))
                    qv(lambda e: e.tensor_tensor(
                        out=fd[0][:, 0:512], in0=fcx[0][:, 0:512],
                        in1=fcx[1][:, 0:512], op=ALU.max))
                    qv(lambda e: e.tensor_tensor(
                        out=fd[1][:, 0:512], in0=fb[0][:, 0:512],
                        in1=fd[0][:, 0:512], op=ALU.add))
                    v_c5y = cop(qv, "vd", lambda e: e.tensor_tensor(
                        out=fe[0][:, 0:512], in0=fd[1][:, 0:512],
                        in1=fd[0][:, 0:512], op=ALU.subtract))
                    qsc.wait("vd", v_c5y)
                    s_zy = cop(qsc, "sd", lambda e, g2=g2: e.activation(
                        out=fe[1][:, 0:512], in_=y5t[g2][:], func=AF.Identity,
                        scale=z32t[:]))
                    qv.wait("sd", s_zy)
                    qv(lambda e: e.tensor_tensor(
                        out=fb[1][:, 0:512], in0=fe[1][:, 0:512],
                        in1=fe[0][:, 0:512], op=ALU.add))
                    qv.wait(*lDr)
                    qv(lambda e, g2=g2: e.tensor_scalar(
                        out=fcx[0][:, 0:512], in0=mmb[g2][:], scalar1=z32t[:],
                        scalar2=None, op0=ALU.mult))
                    qv(lambda e, g2=g2: e.tensor_scalar(
                        out=fcx[1][:, 0:512], in0=mab[g2][:], scalar1=z64t[:],
                        scalar2=None, op0=ALU.mult))
                    qv(lambda e: e.tensor_tensor(
                        out=fd[0][:, 0:512], in0=fcx[0][:, 0:512],
                        in1=fcx[1][:, 0:512], op=ALU.add))
                    v_yd = cop(qv, "vd", lambda e, dout=dout: e.tensor_tensor(
                        out=ew[0][:, dout * 512:(dout + 1) * 512],
                        in0=fb[1][:, 0:512], in1=fd[0][:, 0:512], op=ALU.mult))
                    rdr.release(ts2 * 8 + dout, "vd", v_yd)
                v_all = cnt["vd"]
                v_o_prev = 0
                so_hist = [("so0", 0), ("so0", 0)]
                for tt in range(4 if do_tail else 0):
                    ot = ch * 8 + ts2 * 4 + tt
                    rt2 = tt % 2
                    qt_.wait("vd", v_all)
                    for dout in range(8):
                        tp = 2 + (dout % 2)
                        bank_wait(qt_, tp)
                        t_tr = cop(qt_, "td", lambda e, tp=tp, dout=dout, tt=tt:
                                   e.transpose(
                            pb[tp][:, 0:128],
                            ew[0][:, dout * 512 + tt * 128:dout * 512 + tt * 128 + 128],
                            ident[:]))
                        qv.wait("td", t_tr)
                        v_as = cop(qv, "vd", lambda e, tp=tp, dout=dout, rt2=rt2:
                                   e.tensor_copy(
                            out=fa[rt2][:, dout * 128:(dout + 1) * 128],
                            in_=pb[tp][:, 0:128]))
                        bank_set(tp, ("vd", v_as))
                    if not do_ln3:
                        continue
                    g_nm = ln_simple(fa[rt2][:], rt2, fe[rt2])
                    qsy.wait("vd", cnt["vd"])
                    lX2 = dma(qsy, "lx", lambda e, ot=ot, rt2=rt2: e.dma_start(
                        out=fd[rt2][:], in_=x2_s[ot]))
                    qsc.wait("gd", g_nm)
                    qsc.wait("vd", v_o_prev)
                    s_n3 = cop(qsc, "sd", lambda e, rt2=rt2: e.activation(
                        out=srt[:], in_=fa[rt2][:], func=AF.Identity,
                        scale=rst[rt2][:], bias=nmt[rt2][:]))
                    qv.wait("sd", s_n3)
                    qv.wait(*lX2)
                    qv.wait(*so_hist[tt % 2])
                    v_o = cop(qv, "vd", lambda e, rt2=rt2: e.tensor_tensor(
                        out=fb[rt2][:], in0=srt[:], in1=fd[rt2][:], op=ALU.add))
                    v_o_prev = v_o
                    qg.wait("vd", v_o)
                    sO = dma(qg, "so", lambda e, ot=ot, rt2=rt2: e.dma_start(
                        out=out_d[ot], in_=fb[rt2][:]))
                    so_hist[tt % 2] = sO
            flush()
            c_prev = snap()
            barrier(c_prev)
    flush()
    c_c = snap()

    for i_ in range(STREAMS["sp"]):
        qsy.wait(f"sp{i_}", cnt[f"sp{i_}"])
    qsy.wait("so0", cnt["so0"])
    flush()
    st.close()
    return nc


def _np_masks():
    import jax
    import jax.numpy as jnp
    with jax.default_device(jax.devices("cpu")[0]):
        key = jax.random.key(42, impl="threefry2x32")
        keep32 = np.asarray(jax.random.bernoulli(key, 0.8, (B, S, H)))
        old = jax.config.jax_enable_x64
        try:
            jax.config.update("jax_enable_x64", True)
            k2 = jax.random.key(42, impl="threefry2x32")
            u64 = np.asarray(jax.random.uniform(k2, (B, S, H), dtype=jnp.float64))
        finally:
            jax.config.update("jax_enable_x64", old)
        keep64 = u64 < 0.8
    return keep32, keep64


def _stage(core, x, Wq, Wk, Wv, w1, b1, b2_, w2, masks):
    keep32, keep64 = masks
    d = {}
    d["xall"] = x
    ot_rows = []
    for ot in range(32):
        b_, k_ = divmod(ot, 4)
        sk = 8 * k_ + core
        ot_rows.append(x[b_, 128 * sk:128 * (sk + 1), :])
    d["xown"] = np.ascontiguousarray(np.stack(ot_rows))
    d["wqt"] = np.ascontiguousarray(Wq.T)
    d["wkt"] = np.ascontiguousarray(Wk.T)
    d["wvt"] = np.ascontiguousarray(Wv.T)
    d["w1t"] = np.ascontiguousarray(w1.T)
    d["w2t"] = np.ascontiguousarray(w2.T)
    d["b1d"] = np.ascontiguousarray(b1)
    d["b2d"] = np.ascontiguousarray(b2_)
    jj = np.arange(128)
    tril_t = (jj[None, :] >= jj[:, None]).astype(np.float32)
    mm = np.zeros((32, 128, 512), np.float32)
    ma = np.zeros((32, 128, 512), np.float32)
    for jt in range(32):
        for k_ in range(4):
            sk = 8 * k_ + core
            if jt < sk:
                mm[jt, :, k_ * 128:(k_ + 1) * 128] = 1.0
            elif jt == sk:
                mm[jt, :, k_ * 128:(k_ + 1) * 128] = tril_t
                ma[jt, :, k_ * 128:(k_ + 1) * 128] = np.float32(1e-9) * (1.0 - tril_t)
            else:
                ma[jt, :, k_ * 128:(k_ + 1) * 128] = np.float32(1e-9)
    d["mmul"] = mm.astype(ml_dtypes.bfloat16)
    d["madd"] = ma.astype(ml_dtypes.bfloat16)

    def gather(keep):
        rows = []
        for ot in range(32):
            b_, k_ = divmod(ot, 4)
            sk = 8 * k_ + core
            rows.append(keep[b_, 128 * sk:128 * (sk + 1), :])
        g = np.concatenate(rows, axis=0)
        return np.ascontiguousarray(
            (g.T.astype(np.float32) * 1.25).astype(ml_dtypes.bfloat16))

    d["d32"] = gather(keep32)
    d["d64"] = gather(keep64)
    d["identd"] = np.eye(128, dtype=np.float32)
    return d


def kernel(L0_attn_x, L0_attn_Wq, L0_attn_Wk, L0_attn_Wv,
           L0_ffn_w1, L0_ffn_b1, L0_ffn_w2, L0_ffn_b2):
    x = np.ascontiguousarray(np.asarray(L0_attn_x, np.float32))
    masks = _np_masks()
    in_maps = [_stage(c, x, np.asarray(L0_attn_Wq, np.float32),
                      np.asarray(L0_attn_Wk, np.float32),
                      np.asarray(L0_attn_Wv, np.float32),
                      np.asarray(L0_ffn_w1, np.float32),
                      np.asarray(L0_ffn_b1, np.float32),
                      np.asarray(L0_ffn_b2, np.float32),
                      np.asarray(L0_ffn_w2, np.float32), masks)
               for c in range(NC)]
    nc = build()
    res = run_bass_kernel_spmd(nc, in_maps, core_ids=list(range(NC)))
    out = np.empty((B, S, H), np.float32)
    for c in range(NC):
        oc = np.asarray(res.results[c]["out"])
        for ot in range(32):
            b_, k_ = divmod(ot, 4)
            sk = 8 * k_ + c
            out[b_, 128 * sk:128 * (sk + 1), :] = oc[ot]
    return out
